# version 33
# baseline (speedup 1.0000x reference)
import math
import numpy as np

import concourse.bass as bass
import concourse.mybir as mybir
import concourse.bass_utils as _bu
from concourse.bass_utils import run_bass_kernel_spmd

if not getattr(_bu, "_ldw_patched", False):
    _orig_run_command = _bu.run_command

    def _run_command_ldw(argv, **kw):
        argv = ["--enable-ldw-opt=true" if a == "--enable-ldw-opt=false" else a
                for a in argv]
        return _orig_run_command(argv, **kw)

    _bu.run_command = _run_command_ldw
    _bu._ldw_patched = True

F32 = mybir.dt.float32
BF16 = mybir.dt.bfloat16
FP16 = mybir.dt.float16
I32 = mybir.dt.int32
AF = mybir.ActivationFunctionType
ALU = mybir.AluOpType
AX = mybir.AxisListType

MODE = "split"

T = 2048
D = 1024
DQ = 512
HID = 3072
NT = T // 128
KD = D // 128
KQ = DQ // 128
NH = HID // 128
SLAB = 512
NSLAB = T // SLAB
EPS = 1e-5
C_ATT = 1.0 / math.sqrt(DQ)
C127 = 1.0 / 127.0
C127SQ = 1.0 / (127.0 * 127.0)
NEG = -1.0e30
SA = 64.0
SP = 16.0
SG = 1.0 / 16.0
SD = 64.0
SH = 1.0 / 64.0

ENGINES = ("sy", "gp", "te", "ve", "se")


class Builder:

    def __init__(self):
        self.ops = {e: [] for e in ENGINES}
        self.cnt = {}

    def sem(self, name):
        self.cnt[name] = 0

    def wait(self, eng, sem, val):
        assert val <= self.cnt[sem], (eng, sem, val, self.cnt[sem])
        if val > 0:
            self.ops[eng].append(("w", sem, val))

    def op(self, eng, fn, inc=None, amt=1):
        self.ops[eng].append(("c", fn, inc, amt))
        if inc is not None:
            self.cnt[inc] += amt

    def drain_inc(self, eng, sem, amt=1):
        self.ops[eng].append(("d", sem, amt))
        self.cnt[sem] += amt

    def dma(self, eng, out_ap_fn, in_ap_fn, sem):
        def fn(e):
            return e.dma_start(out=out_ap_fn(), in_=in_ap_fn())
        self.op(eng, fn, inc=sem, amt=16)


def build_program(mode):
    split = mode == "split"
    nc = bass.Bass("TRN2", target_bir_lowering=False)

    x_d = nc.dram_tensor("x", [T, D], F32, kind="ExternalInput")
    wqkvT_d = nc.dram_tensor("wqkvT", [D, 3 * DQ], F32, kind="ExternalInput")
    woutT_d = nc.dram_tensor("woutT", [DQ, D], F32, kind="ExternalInput")
    brow_d = nc.dram_tensor("brow", [1, D], F32, kind="ExternalInput")
    wgT_d = nc.dram_tensor("wgT", [D, HID], I32, kind="ExternalInput")
    wuT_d = nc.dram_tensor("wuT", [D, HID], I32, kind="ExternalInput")
    wdT_d = nc.dram_tensor("wdT", [HID, D], I32, kind="ExternalInput")
    ident_d = nc.dram_tensor("ident", [128, 128], F32, kind="ExternalInput")
    mask_d = nc.dram_tensor("mask", [128, 128], F32, kind="ExternalInput")
    out_d = nc.dram_tensor("out", [T, D], F32, kind="ExternalOutput")
    sscr_d = nc.dram_tensor("sscr", [T, D], F32)
    WT = FP16 if split else F32
    wgb_d = nc.dram_tensor("wgb_scr", [D, HID], WT)
    wub_d = nc.dram_tensor("wub_scr", [D, HID], WT)
    wdb_d = nc.dram_tensor("wdb_scr", [HID, D], WT)

    SBUF_BASE = 16896
    off = [SBUF_BASE]

    def place(name, shape, dtype, at=None):
        nbytes = int(np.prod(shape[1:])) * mybir.dt.size(dtype)
        o = off[0] if at is None else at
        o = (o + 31) // 32 * 32
        t = nc.alloc_sbuf_tensor_at(name, list(shape), dtype, offset=o)
        end = o + nbytes
        off[0] = max(off[0], end)
        return t

    ident = place("identf", [128, 128], F32)
    maskt = place("maskt", [128, 128], F32)
    bbc_off = (off[0] + 31) // 32 * 32
    b_bc = place("b_bc", [128, D], F32)
    brow = place("brow_s", [1, D], F32, at=bbc_off)
    ones1 = place("ones1", [1, 128], F32)
    st12 = place("st12", [128, 2, 12], F32)
    mv = place("mv", [128, 2, 2], F32)
    stdv = place("stdv", [128, 2], F32)
    inv = place("inv", [128, 2], F32)
    negmi = place("negmi", [128, 2], F32)
    negmax = place("negmax", [128, 1], F32)
    negcm = place("negcm", [128, 1], F32)
    rsum = place("rsum", [128, 1], F32)
    invr = place("invr", [128, 1], F32)
    epsb = place("epsb", [128, 1], F32)
    zeroc = place("zeroc", [128, 1], F32)

    if split:
        xnT = (place("xnT_hi", [128, KD, T], FP16),
               place("xnT_lo", [128, KD, T], FP16))
    else:
        xnT = (place("xnT", [128, KD, T], F32),)

    r2_base = off[0]
    if split:
        qT = (place("qT_hi", [128, KQ, T], FP16), place("qT_lo", [128, KQ, T], FP16))
        kT = (place("kT_hi", [128, KQ, T], FP16), place("kT_lo", [128, KQ, T], FP16))
        vtk = (place("v_hi", [128, NT, DQ], FP16), place("v_lo", [128, NT, DQ], FP16))
    else:
        qT = (place("qT", [128, KQ, T], F32),)
        kT = (place("kT", [128, KQ, T], F32),)
        vtk = (place("vtk", [128, NT, DQ], F32),)
    r2_end_a = off[0]
    wvf = place("wvf", [128, KD, DQ], F32, at=r2_base + 32768)
    if split:
        wvh = place("wvh", [128, KD, DQ], FP16, at=r2_base + 49152)
        wvl = place("wvl", [128, KD, DQ], FP16, at=r2_base + 57344)
    off[0] = r2_base
    if split:
        hT = (place("hT_hi", [128, NH, SLAB], FP16, at=r2_base),
              place("hT_lo", [128, NH, SLAB], FP16))
        wgb = place("wgb", [128, 2, KD, 128], FP16)
        wub = place("wub", [128, 2, KD, 128], FP16)
        wdb = place("wdb", [128, 2, D], FP16)
    else:
        hT = (place("hT", [128, NH, SLAB], F32, at=r2_base),)
        wgb = place("wgb", [128, 2, KD, 128], F32)
        wub = place("wub", [128, 2, KD, 128], F32)
        wdb = place("wdb", [128, 2, D], F32)
    off[0] = max(off[0], r2_end_a)

    r3_base = off[0]
    r3 = r3_base

    def at(kb):
        return r3 + int(kb * 1024)

    xt = place("xt", [128, 2, D], F32, at=at(0))
    xnsc = place("xnsc", [128, 2, D], F32, at=at(8))
    pfull = place("pfull", [128, T], F32, at=at(0))
    if split:
        pT = (place("pT_hi", [128, NT, 128], FP16, at=at(8)),
              place("pT_lo", [128, NT, 128], FP16, at=at(12)))
    else:
        pT = (place("pT", [128, NT, 128], F32, at=at(8)),)

    wqf = place("wqf", [128, 2, KD, 128], F32, at=at(16))
    if split:
        wqh = place("wqh", [128, 2, KD, 128], FP16, at=at(24))
        wql = place("wql", [128, 2, KD, 128], FP16, at=at(28))
        wo = (place("wo_hi", [128, KQ, D], FP16, at=at(16)),
              place("wo_lo", [128, KQ, D], FP16, at=at(24)))
    else:
        wof = place("wof", [128, KQ, D], F32, at=at(16))
        wo = (wof,)

    if split:
        oaT = (place("oaT_hi", [128, KQ, 128], FP16, at=at(33)),
               place("oaT_lo", [128, KQ, 128], FP16, at=at(34)))
    else:
        oaT = (place("oaT", [128, KQ, 128], F32, at=at(33)),)
    oaf = place("oaf", [128, DQ], F32, at=at(35))
    stl = place("stl", [128, D], F32, at=at(37))
    if split:
        wof = place("wof_st", [128, 1, D], F32, at=at(37))
        wof2 = place("wof_st2", [128, 1, D], F32, at=at(33))
    sin = place("sin", [128, 2, D], F32, at=at(0))
    xin = place("xin", [128, 2, D], F32, at=at(8))
    otl = place("otl", [128, 2, D], F32, at=at(16))
    gtl = place("gtl", [128, D], F32, at=at(24))
    hsc = place("hsc", [128, SLAB], F32, at=at(28))
    if split:
        hfs = place("hfs", [128, SLAB], F32, at=at(30))
    gtl2 = place("gtl2", [128, D], F32, at=at(37))
    off[0] = max(off[0], at(41))
    assert off[0] <= 229200, f"SBUF over budget: {off[0]}"

    psA = nc.alloc_psum_tensor("psA", [128, 2048], F32)
    psB = nc.alloc_psum_tensor("psB", [128, 1024], F32)
    psC = nc.alloc_psum_tensor("psC", [128, 1024], F32)

    k = Builder()
    for s in ("dC", "dG", "dX0", "dX1", "dWq0", "dWq1", "dWv", "dWo", "vw", "dS", "dL0", "dL1", "dO0", "dO1",
              "dF0", "dF1", "dD0", "dD1", "pe", "ve", "se"):
        k.sem(s)


    k.dma("sy", lambda: ident.ap(), lambda: ident_d.ap(), "dC")
    k.dma("sy", lambda: maskt.ap(), lambda: mask_d.ap(), "dC")
    k.dma("sy", lambda: brow.ap(), lambda: brow_d.ap(), "dC")
    k.wait("ve", "dC", 48)
    k.op("ve", lambda e: e.memset(ones1.ap(), 1.0))
    k.op("ve", lambda e: e.memset(epsb.ap(), EPS))
    k.op("ve", lambda e: e.memset(zeroc.ap(), 0.0))
    k.drain_inc("ve", "ve")
    ve_ones = k.cnt["ve"]
    k.wait("te", "dC", 48)
    k.wait("te", "ve", ve_ones)
    for n in range(2):
        k.op("te", lambda e, n=n: e.matmul(
            psB.ap()[:, n * 512:(n + 1) * 512], ones1.ap(),
            brow.ap()[:, n * 512:(n + 1) * 512], start=True, stop=True),
            inc=("pe" if n == 1 else None))
    k.wait("ve", "pe", k.cnt["pe"])
    k.op("ve", lambda e: e.tensor_copy(b_bc.ap(), psB.ap()))
    k.drain_inc("ve", "ve")
    ve_bbc = k.cnt["ve"]

    def mm_group(psum_ap_fn, terms, inc="pe"):
        n = len(terms)
        for j, (lf, rf) in enumerate(terms):
            k.op("te", lambda e, lf=lf, rf=rf, j=j, n=n, pf=psum_ap_fn: e.matmul(
                pf(), lf(), rf(), start=(j == 0), stop=(j == n - 1)),
                inc=(inc if j == n - 1 else None))

    def passes(a_bufs, b_bufs):
        if len(a_bufs) == 1 and len(b_bufs) == 1:
            return [(a_bufs[0], b_bufs[0])]
        if len(a_bufs) == 1:
            return [(a_bufs[0], b_bufs[0]), (a_bufs[0], b_bufs[1])]
        if len(b_bufs) == 1:
            return [(a_bufs[0], b_bufs[0]), (a_bufs[1], b_bufs[0])]
        return [(a_bufs[0], b_bufs[0]), (a_bufs[0], b_bufs[1]), (a_bufs[1], b_bufs[0])]

    x_r = x_d.ap().rearrange("(i p) d -> i p d", p=128)
    wq_r = wqkvT_d.ap().rearrange("(c p) e -> p c e", p=128)
    se_lnapply = [None] * NT
    ve_xn_evac = [None] * NT
    pe_xn_tr = [None] * NT
    pe_xn_tr_half = [None] * NT
    ve_wsplit = [None] * 8
    te_wuse = [None] * 8
    ve_evac_qk = {}
    qk_rr = [0]

    for i in range(2):
        k.dma("sy", lambda i=i: xt.ap()[:, i % 2, :], lambda i=i: x_r[i], f"dX{i % 2}")
    k.dma("sy", lambda: wvf.ap(), lambda: wq_r[:, :, 1024:1536], "dWv")

    def emit_wq_load(mi):
        if mi >= 2:
            if split:
                k.wait("sy", "ve", ve_wsplit[mi - 2])
            else:
                k.wait("sy", "pe", te_wuse[mi - 2])
        k.dma("sy", lambda mi=mi: wqf.ap()[:, mi % 2],
              lambda mi=mi: wq_r[:, :, mi * 128:(mi + 1) * 128], f"dWq{mi % 2}")

    def emit_wsplit(mi):
        if split:
            k.wait("ve", f"dWq{mi % 2}", 16 * (mi // 2 + 1))
            if mi >= 2:
                k.wait("ve", "pe", te_wuse[mi - 2])
            k.op("ve", lambda e, mi=mi: e.tensor_scalar_mul(wqh.ap()[:, mi % 2],
                                                            wqf.ap()[:, mi % 2], SA))
            k.op("ve", lambda e: e.drain())
            k.op("ve", lambda e, mi=mi: e.scalar_tensor_tensor(
                wql.ap()[:, mi % 2], wqf.ap()[:, mi % 2], SA, wqh.ap()[:, mi % 2],
                ALU.mult, ALU.subtract))
            k.drain_inc("ve", "ve")
        ve_wsplit[mi] = k.cnt["ve"]

    def emit_wv_split(kc_lo, kc_hi):
        if not split:
            return
        k.wait("ve", "dWv", 16)
        k.op("ve", lambda e, kc_lo=kc_lo, kc_hi=kc_hi: e.tensor_scalar_mul(
            wvh.ap()[:, kc_lo:kc_hi], wvf.ap()[:, kc_lo:kc_hi], SA))
        k.op("ve", lambda e: e.drain())
        k.op("ve", lambda e, kc_lo=kc_lo, kc_hi=kc_hi: e.scalar_tensor_tensor(
            wvl.ap()[:, kc_lo:kc_hi], wvf.ap()[:, kc_lo:kc_hi], SA,
            wvh.ap()[:, kc_lo:kc_hi], ALU.mult, ALU.subtract))
        k.drain_inc("ve", "ve")

    ve_wv_split = [0]
    ve_evac_v = ve_evac_qk

    def emit_v_group(hh, tc, xn_ready):
        k.wait("te", xn_ready[0], xn_ready[1])
        if split:
            k.wait("te", "ve", ve_wv_split[0])
            wvb = [lambda kc: wvh.ap()[:, kc, hh * 256:(hh + 1) * 256],
                   lambda kc: wvl.ap()[:, kc, hh * 256:(hh + 1) * 256]]
        else:
            k.wait("te", "dWv", 16)
            wvb = [lambda kc: wvf.ap()[:, kc, hh * 256:(hh + 1) * 256]]
        bank = qk_rr[0] % 4
        qk_rr[0] += 1
        prev = ve_evac_qk.get(bank)
        if prev is not None:
            k.wait("te", prev[0], prev[1])
        terms = []
        for kc in range(KD):
            for (xb, wv) in passes([
                    (lambda t: (lambda kc, tc=tc: t.ap()[:, kc, tc * 128:(tc + 1) * 128]))(t)
                    for t in xnT], wvb):
                terms.append(((lambda xb=xb, kc=kc: xb(kc)),
                              (lambda wv=wv, kc=kc: wv(kc))))
        mm_group(lambda bank=bank: psA.ap()[:, bank * 512: bank * 512 + 256], terms)
        pe_g = k.cnt["pe"]
        k.wait("se", "pe", pe_g)
        k.op("se", lambda e, tc=tc, hh=hh, bank=bank: e.activation(
            vtk[0].ap()[:, tc, hh * 256:(hh + 1) * 256],
            psA.ap()[:, bank * 512: bank * 512 + 256], AF.Identity, bias=zeroc.ap()))
        k.drain_inc("se", "se")
        se_c = k.cnt["se"]
        if split:
            k.wait("ve", "se", se_c)
            k.op("ve", lambda e, tc=tc, hh=hh, bank=bank: e.tensor_tensor(
                vtk[1].ap()[:, tc, hh * 256:(hh + 1) * 256],
                psA.ap()[:, bank * 512: bank * 512 + 256],
                vtk[0].ap()[:, tc, hh * 256:(hh + 1) * 256], ALU.subtract))
            k.drain_inc("ve", "ve")
            ve_evac_qk[bank] = ("ve", k.cnt["ve"])
        else:
            ve_evac_qk[bank] = ("se", se_c)

    def emit_qk_group(mi, n, xn_ready):
        k.wait("te", xn_ready[0], xn_ready[1])
        if split:
            k.wait("te", "ve", ve_wsplit[mi])
            wbufs = [lambda kc, mi=mi: wqh.ap()[:, mi % 2, kc, :],
                     lambda kc, mi=mi: wql.ap()[:, mi % 2, kc, :]]
        else:
            k.wait("te", f"dWq{mi % 2}", 16 * (mi // 2 + 1))
            wbufs = [lambda kc, mi=mi: wqf.ap()[:, mi % 2, kc, :]]
        bank = qk_rr[0] % 4
        qk_rr[0] += 1
        prev = ve_evac_qk.get(bank)
        if prev is not None:
            k.wait("te", prev[0], prev[1])
        terms = []
        for kc in range(KD):
            for (wa, xb) in passes(wbufs, [
                    (lambda t: (lambda kc, n: t.ap()[:, kc, n * 512:(n + 1) * 512]))(t)
                    for t in xnT]):
                terms.append(((lambda wa=wa, kc=kc: wa(kc)),
                              (lambda xb=xb, kc=kc, n=n: xb(kc, n))))
        mm_group(lambda bank=bank: psA.ap()[:, bank * 512:(bank + 1) * 512], terms)
        pe_g = k.cnt["pe"]
        dst = qT if mi < 4 else kT
        m = mi % 4
        k.wait("se", "pe", pe_g)
        k.op("se", lambda e, dst=dst, m=m, n=n, bank=bank: e.activation(
            dst[0].ap()[:, m, n * 512:(n + 1) * 512],
            psA.ap()[:, bank * 512:(bank + 1) * 512], AF.Identity, bias=zeroc.ap()))
        k.drain_inc("se", "se")
        se_c = k.cnt["se"]
        if split:
            k.wait("ve", "se", se_c)
            k.op("ve", lambda e, dst=dst, m=m, n=n, bank=bank: e.tensor_tensor(
                dst[1].ap()[:, m, n * 512:(n + 1) * 512],
                psA.ap()[:, bank * 512:(bank + 1) * 512],
                dst[0].ap()[:, m, n * 512:(n + 1) * 512], ALU.subtract))
            k.drain_inc("ve", "ve")
            ve_evac_qk[bank] = ("ve", k.cnt["ve"])
        else:
            ve_evac_qk[bank] = ("se", se_c)

    emit_wq_load(0)
    emit_wq_load(1)

    def emit_ln_chain(j):
        p = j % 2
        k.wait("ve", f"dX{p}", 16 * (j // 2 + 1))
        if j >= 2:
            k.wait("ve", "se", se_lnapply[j - 2])
        k.op("ve", lambda e, p=p: e.bn_stats(st12.ap()[:, p, 0:6], xt.ap()[:, p, 0:512]))
        k.op("ve", lambda e, p=p: e.bn_stats(st12.ap()[:, p, 6:12], xt.ap()[:, p, 512:1024]))
        k.op("ve", lambda e: e.drain())
        k.op("ve", lambda e, p=p: e.bn_aggr(mv.ap()[:, p, :], st12.ap()[:, p, :]))
        k.drain_inc("ve", "ve")
        ve_mv = k.cnt["ve"]
        k.wait("se", "ve", ve_mv)
        k.op("se", lambda e, p=p: e.activation(stdv.ap()[:, p:p + 1], mv.ap()[:, p, 1:2],
                                               AF.Sqrt, bias=epsb.ap()))
        k.drain_inc("se", "se")
        se_std = k.cnt["se"]
        k.wait("ve", "se", se_std)
        k.op("ve", lambda e, p=p: e.reciprocal(inv.ap()[:, p:p + 1], stdv.ap()[:, p:p + 1]))
        k.op("ve", lambda e: e.drain())
        k.op("ve", lambda e, p=p: e.tensor_scalar(negmi.ap()[:, p:p + 1], mv.ap()[:, p, 0:1],
                                                  inv.ap()[:, p:p + 1], -1.0,
                                                  ALU.mult, ALU.mult))
        k.drain_inc("ve", "ve")
        ve_ln = k.cnt["ve"]
        k.wait("se", "ve", ve_ln)
        if j >= 2:
            k.wait("se", "pe", pe_xn_tr[j - 2])
        k.op("se", lambda e, p=p: e.activation(xnsc.ap()[:, p, :], xt.ap()[:, p, :],
                                               AF.Identity, bias=negmi.ap()[:, p:p + 1],
                                               scale=inv.ap()[:, p:p + 1]))
        k.drain_inc("se", "se")
        se_lnapply[j] = k.cnt["se"]
        if j + 2 < NT:
            k.wait("sy", "se", se_lnapply[j])
            k.dma("sy", lambda j=j: xt.ap()[:, j % 2, :], lambda j=j: x_r[j + 2], f"dX{j % 2}")

    def emit_xn_evac(j):
        pe_need = pe_xn_tr_half[j]
        ps = psC if j % 2 == 0 else psB
        for half in range(2):
            k.wait("se", "pe", pe_need[half])
            src = ps.ap().rearrange("p (h c n) -> p h c n", h=2, c=4)[:, half]
            lo_s = half * 4
            k.op("se", lambda e, j=j, lo_s=lo_s, src=src: e.activation(
                xnT[0].ap()[:, lo_s:lo_s + 4, j * 128:(j + 1) * 128], src,
                AF.Identity, bias=zeroc.ap()))
            k.drain_inc("se", "se")
            se_c = k.cnt["se"]
            if split:
                k.wait("ve", "se", se_c)
                k.op("ve", lambda e, j=j, lo_s=lo_s, src=src: e.tensor_tensor(
                    xnT[1].ap()[:, lo_s:lo_s + 4, j * 128:(j + 1) * 128], src,
                    xnT[0].ap()[:, lo_s:lo_s + 4, j * 128:(j + 1) * 128], ALU.subtract))
                k.drain_inc("ve", "ve")
        ve_xn_evac[j] = ("ve", k.cnt["ve"]) if split else ("se", k.cnt["se"])

    emit_ln_chain(0)
    emit_ln_chain(1)
    emit_wv_split(0, 4)
    emit_wv_split(4, 8)
    ve_wv_split[0] = k.cnt["ve"]
    for i in range(NT):
        k.wait("te", "se", se_lnapply[i])
        ps = psC if i % 2 == 0 else psB
        if i == 1:
            k.wait("te", "ve", ve_bbc)
        pe_need = [None, None]
        for half in range(2):
            if i >= 2:
                k.wait("te", ve_xn_evac[i - 2][0], ve_xn_evac[i - 2][1])
            for c4 in range(4):
                c = half * 4 + c4
                k.op("te", lambda e, half=half, c4=c4, c=c, p=i % 2, ps=ps: e.transpose(
                    ps.ap()[:, half * 512 + c4 * 128: half * 512 + (c4 + 1) * 128],
                    xnsc.ap()[:, p, c * 128:(c + 1) * 128], ident.ap()),
                    inc=("pe" if c4 == 3 else None))
            pe_need[half] = k.cnt["pe"]
        pe_xn_tr_half[i] = pe_need
        pe_xn_tr[i] = k.cnt["pe"]
        if i >= 1 and (i - 1) % 4 == 3:
            n = (i - 1) // 4
            emit_qk_group(0, n, ve_xn_evac[i - 1])
            emit_qk_group(1, n, ve_xn_evac[i - 1])
        if i == 2:
            emit_v_group(0, 0, ve_xn_evac[0])
            emit_v_group(1, 0, ve_xn_evac[0])
        if 2 <= i <= 15:
            tc = i - 1
            emit_v_group(0, tc, ve_xn_evac[tc])
            emit_v_group(1, tc, ve_xn_evac[tc])
        emit_xn_evac(i)
        if i + 2 < NT:
            emit_ln_chain(i + 2)
        if i == 1:
            emit_wsplit(0)
            emit_wsplit(1)
    emit_qk_group(0, 3, ve_xn_evac[NT - 1])
    emit_qk_group(1, 3, ve_xn_evac[NT - 1])
    emit_v_group(0, NT - 1, ve_xn_evac[NT - 1])
    emit_v_group(1, NT - 1, ve_xn_evac[NT - 1])
    te_wuse[0] = te_wuse[1] = k.cnt["pe"]
    ve_xn_done = k.cnt["ve"]


    emit_wq_load(2)
    emit_wq_load(3)
    xn_all = ve_xn_evac[NT - 1]
    emit_wsplit(2)
    for mi in range(2, 8):
        if mi + 1 < 8:
            emit_wsplit(mi + 1)
        for n in range(4):
            emit_qk_group(mi, n, xn_all)
        te_wuse[mi] = k.cnt["pe"]
        if mi + 2 < 8:
            emit_wq_load(mi + 2)
    ve_p1_done = k.cnt["ve"]
    se_p1_done = k.cnt["se"]

    se_exp = [None] * NT
    pe_ptr_done = [None] * NT
    pe_scores_l = [None] * NT
    ve_sevac_cnt = None
    last_psC_evac = {0: ("ve", k.cnt["ve"]), 1: ("ve", k.cnt["ve"])}
    psb5_free = [None]
    ve_sevac_b4 = None

    def emit_scores(i):
        ncols = (i + 1) * 128
        nblk = (ncols + 511) // 512
        if prv.get(i) is None:
            k.wait("te", "ve", ve_p1_done)
            k.wait("te", "se", se_p1_done)
        else:
            k.wait("te", "se", se_exp[prv[i]])
        for b in range(nblk):
            nb = min(512, ncols - b * 512)
            terms = []
            for kc in range(KQ):
                for (qa, ka) in passes(
                        [(lambda t: (lambda kc, i=i: t.ap()[:, kc, i * 128:(i + 1) * 128]))(t)
                         for t in qT],
                        [(lambda t: (lambda kc, b=b, nb=nb: t.ap()[:, kc, b * 512: b * 512 + nb]))(t)
                         for t in kT]):
                    terms.append(((lambda qa=qa, kc=kc: qa(kc)),
                                  (lambda ka=ka, kc=kc: ka(kc))))
            mm_group(lambda b=b, nb=nb: psA.ap()[:, b * 512: b * 512 + nb], terms)
        pe_scores_l[i] = k.cnt["pe"]

    def softprep_exp(j):
        ncols = (j + 1) * 128
        k.wait("ve", "pe", pe_scores_l[j])
        k.op("ve", lambda e, j=j: e.tensor_tensor(
            psA.ap()[:, j * 128:(j + 1) * 128], psA.ap()[:, j * 128:(j + 1) * 128],
            maskt.ap(), ALU.add))
        k.op("ve", lambda e: e.drain())
        k.op("ve", lambda e, ncols=ncols: e.reduce_max(
            negmax.ap(), psA.ap()[:, 0:ncols], axis=AX.X, negate=True))
        k.op("ve", lambda e: e.drain())
        k.op("ve", lambda e: e.tensor_scalar_mul(negcm.ap(), negmax.ap(),
                                                 C_ATT / (SA * SA) if split else C_ATT))
        k.drain_inc("ve", "ve")
        ve_sm = k.cnt["ve"]
        k.wait("se", "ve", ve_sm)
        if prv.get(j) is not None:
            k.wait("se", "pe", pe_ptr_done[prv[j]])
        k.op("se", lambda e, ncols=ncols: e.activation(
            pfull.ap()[:, 0:ncols], psA.ap()[:, 0:ncols], AF.Exp,
            bias=negcm.ap(), scale=(C_ATT / (SA * SA) if split else C_ATT),
            accum_out=rsum.ap()))
        k.drain_inc("se", "se")
        se_exp[j] = k.cnt["se"]

    tile_order = []
    for z in range(NT // 2):
        tile_order += [z, NT - 1 - z]
    store_pos = {g: idx for idx, g in enumerate(tile_order)}
    nxt = {tile_order[z]: tile_order[z + 1] for z in range(NT - 1)}
    prv = {tile_order[z + 1]: tile_order[z] for z in range(NT - 1)}

    emit_scores(tile_order[0])
    softprep_exp(tile_order[0])

    wo_r = woutT_d.ap().rearrange("(c p) o -> p c o", p=128)
    if split:
        stgs = [wof, wof2]
        for hw in range(KQ):
            if hw >= 2:
                k.wait("se", "vw", hw - 1)
            k.dma("se", lambda hw=hw: stgs[hw % 2].ap(),
                  lambda hw=hw: wo_r[:, hw: hw + 1, :], "dWo")
            k.wait("se", "dWo", k.cnt["dWo"])
            k.wait("se", "ve", ve_wsplit[7])
            k.op("se", lambda e, hw=hw: e.activation(
                wo[0].ap()[:, hw: hw + 1, :], stgs[hw % 2].ap(), AF.Identity,
                bias=zeroc.ap(), scale=SD))
            k.drain_inc("se", "se")
            se_c = k.cnt["se"]
            k.wait("ve", "se", se_c)
            k.wait("ve", "pe", te_wuse[7])
            k.op("ve", lambda e, hw=hw: e.scalar_tensor_tensor(
                wo[1].ap()[:, hw: hw + 1, :], stgs[hw % 2].ap(), SD,
                wo[0].ap()[:, hw: hw + 1, :], ALU.mult, ALU.subtract))
            k.drain_inc("ve", "vw")
        dW_wo = k.cnt["dWo"]
    else:
        k.wait("se", "pe", te_wuse[7])
        k.dma("se", lambda: wof.ap(), lambda: wo_r, "dWo")
        dW_wo = k.cnt["dWo"]
    vw_done = k.cnt["vw"]

    for i in tile_order:
        ncols = (i + 1) * 128
        k.wait("ve", "se", se_exp[i])
        k.op("ve", lambda e: e.reciprocal(invr.ap(), rsum.ap()))
        if split:
            k.op("ve", lambda e: e.drain())
            k.op("ve", lambda e: e.tensor_scalar_mul(
                invr.ap(), invr.ap(), 1.0 / (SA * SP * SG * SD)))
        k.drain_inc("ve", "ve")
        if nxt.get(i) is not None:
            emit_scores(nxt[i])
        k.wait("te", "se", se_exp[i])
        pT_ready = []
        tr_slots = [psC.ap()[:, 0:512], psC.ap()[:, 512:1024], psB.ap()[:, 512:1024]]
        tr_free = [last_psC_evac[0], last_psC_evac[1], psb5_free[0]]
        for g in range((i + 4) // 4):
            sl = g % 3
            if tr_free[sl] is not None:
                k.wait("te", tr_free[sl][0], tr_free[sl][1])
            if sl == 2 and ve_sevac_cnt is not None:
                k.wait("te", "ve", ve_sevac_cnt)
            jmax = min(4, i + 1 - g * 4)
            for j4 in range(jmax):
                j = g * 4 + j4
                k.op("te", lambda e, sl=sl, j4=j4, j=j: e.transpose(
                    tr_slots[sl][:, j4 * 128:(j4 + 1) * 128],
                    pfull.ap()[:, j * 128:(j + 1) * 128], ident.ap()),
                    inc=("pe" if j4 == jmax - 1 else None))
            pe_tr = k.cnt["pe"]
            src = tr_slots[sl].rearrange("p (c n) -> p c n", c=4)[:, 0:jmax]
            g4 = g * 4
            k.wait("se", "pe", pe_tr)
            k.op("se", lambda e, g4=g4, jmax=jmax, src=src: e.activation(
                pT[0].ap()[:, g4: g4 + jmax, :], src, AF.Identity, bias=zeroc.ap(),
                scale=(SP if split else 1.0)))
            k.drain_inc("se", "se")
            se_c = k.cnt["se"]
            if split:
                k.wait("ve", "se", se_c)
                k.op("ve", lambda e, g4=g4, jmax=jmax, src=src: e.scalar_tensor_tensor(
                    pT[1].ap()[:, g4: g4 + jmax, :], src, SP,
                    pT[0].ap()[:, g4: g4 + jmax, :], ALU.mult, ALU.subtract))
                k.drain_inc("ve", "ve")
                done = ("ve", k.cnt["ve"])
            else:
                done = ("se", se_c)
            tr_free[sl] = done
            if sl < 2:
                last_psC_evac[sl] = done
            else:
                psb5_free[0] = done
            pT_ready.append(done)
        pe_ptr_done[i] = k.cnt["pe"]
        if ve_sevac_cnt is not None:
            k.wait("te", "ve", ve_sevac_b4)
        else:
            k.wait("te", ve_xn_evac[NT - 1][0], ve_xn_evac[NT - 1][1])
        nmm = (i + 1) * (3 if split else 1)
        emitted = 0
        for j in range(i + 1):
            if j % 4 == 0:
                rdy = pT_ready[j // 4]
                k.wait("te", rdy[0], rdy[1])
            for (pa, va) in passes(
                    [(lambda t: (lambda j: t.ap()[:, j, :]))(t) for t in pT],
                    [(lambda t: (lambda j: t.ap()[:, j, :]))(t) for t in vtk]):
                k.op("te", lambda e, pa=pa, va=va, j=j,
                     first=(emitted == 0), last=(emitted == nmm - 1): e.matmul(
                         psB.ap()[:, 0:512], pa(j), va(j), start=first, stop=last),
                     inc=("pe" if emitted == nmm - 1 else None))
                emitted += 1
        pe_av = k.cnt["pe"]
        k.wait("se", "pe", pe_av)
        k.op("se", lambda e: e.activation(oaf.ap(), psB.ap()[:, 0:512], AF.Identity,
                                          bias=zeroc.ap()))
        k.drain_inc("se", "se")
        se_oaf = k.cnt["se"]
        k.wait("te", "se", se_oaf)
        oat_slots = [psC.ap()[:, 0:512], psC.ap()[:, 512:1024], psB.ap()[:, 512:1024]]
        slot_free = [last_psC_evac[0], last_psC_evac[1], psb5_free[0]]
        slot_evac = [None] * 3
        oa_evac_m = [None] * KQ
        for m in range(KQ):
            sl = m % 3
            if slot_free[sl] is not None:
                k.wait("te", slot_free[sl][0], slot_free[sl][1])
            k.op("te", lambda e, m=m, sl=sl: e.transpose(
                oat_slots[sl][:, 0:128],
                oaf.ap()[:, m * 128:(m + 1) * 128], ident.ap()), inc="pe")
            pe_oat = k.cnt["pe"]
            k.wait("se", "pe", pe_oat)
            k.op("se", lambda e, m=m, sl=sl: e.activation(
                oaT[0].ap()[:, m, :], oat_slots[sl][:, 0:128],
                AF.Identity, bias=zeroc.ap(), scale=(SG if split else 1.0)))
            k.drain_inc("se", "se")
            se_c = k.cnt["se"]
            if split:
                k.wait("ve", "se", se_c)
                k.op("ve", lambda e, m=m, sl=sl: e.scalar_tensor_tensor(
                    oaT[1].ap()[:, m, :], oat_slots[sl][:, 0:128], SG,
                    oaT[0].ap()[:, m, :], ALU.mult, ALU.subtract))
                k.drain_inc("ve", "ve")
                slot_evac[sl] = ("ve", k.cnt["ve"])
            else:
                slot_evac[sl] = ("se", se_c)
            slot_free[sl] = slot_evac[sl]
            oa_evac_m[m] = slot_evac[sl]
        last_psC_evac[0] = slot_evac[0]
        last_psC_evac[1] = slot_evac[1]
        if slot_evac[2] is not None:
            psb5_free[0] = slot_evac[2]
        if nxt.get(i) is not None:
            softprep_exp(nxt[i])
        if ve_sevac_cnt is None:
            if split:
                k.wait("te", "vw", vw_done)
            else:
                k.wait("te", "dWo", dW_wo)
        npass_o = 3 if split else 1
        for n2 in range(2):
            emitted = 0
            nmm = KQ * npass_o
            for kc in range(KQ):
                if n2 == 0:
                    rdy = oa_evac_m[kc]
                    k.wait("te", rdy[0], rdy[1])
                elif kc == 0:
                    k.wait("te", oa_evac_m[2][0], oa_evac_m[2][1])
                for (oa, wa) in passes(
                        [(lambda t: (lambda kc: t.ap()[:, kc, :]))(t) for t in oaT],
                        [(lambda t: (lambda kc, n2=n2: t.ap()[:, kc, n2 * 512:(n2 + 1) * 512]))(t)
                         for t in wo]):
                    k.op("te", lambda e, oa=oa, wa=wa, kc=kc, n2=n2,
                         first=(emitted == 0), last=(emitted == nmm - 1): e.matmul(
                             psB.ap()[:, n2 * 512:(n2 + 1) * 512], oa(kc), wa(kc),
                             start=first, stop=last),
                         inc=("pe" if emitted == nmm - 1 else None))
                    emitted += 1
        pe_s = k.cnt["pe"]
        k.wait("ve", "pe", pe_s)
        if store_pos[i] >= 1:
            k.wait("ve", "dS", 16 * store_pos[i])
        k.op("ve", lambda e: e.scalar_tensor_tensor(
            stl.ap()[:, 0:512], psB.ap()[:, 0:512], invr.ap(), b_bc.ap()[:, 0:512],
            ALU.mult, ALU.add))
        k.drain_inc("ve", "ve")
        ve_sevac_b4 = k.cnt["ve"]
        k.op("ve", lambda e: e.scalar_tensor_tensor(
            stl.ap()[:, 512:1024], psB.ap()[:, 512:1024], invr.ap(),
            b_bc.ap()[:, 512:1024], ALU.mult, ALU.add))
        k.drain_inc("ve", "ve")
        ve_sevac_cnt = k.cnt["ve"]
        k.wait("sy", "ve", ve_sevac_cnt)
        k.dma("sy", lambda i=i: sscr_d.ap()[i * 128:(i + 1) * 128, :], lambda: stl.ap(), "dS")
        sp = store_pos[i]
        if sp < 12:
            k.wait("gp", "dS", 16 * (sp + 1))
            if sp < 4:
                k.dma("gp", lambda sp=sp: wgb_d.ap()[sp * 256:(sp + 1) * 256, :],
                      lambda sp=sp: wgT_d.ap()[sp * 256:(sp + 1) * 256, :], "dG")
            elif sp < 8:
                j = sp - 4
                k.dma("gp", lambda j=j: wub_d.ap()[j * 256:(j + 1) * 256, :],
                      lambda j=j: wuT_d.ap()[j * 256:(j + 1) * 256, :], "dG")
            else:
                j = sp - 8
                k.dma("gp", lambda j=j: wdb_d.ap()[j * 768:(j + 1) * 768, :],
                      lambda j=j: wdT_d.ap()[j * 768:(j + 1) * 768, :], "dG")
    ve_p2_done = ve_sevac_cnt

    wg_r = wgb_d.ap().rearrange("(c p) h -> p c h", p=128)
    wu_r = wub_d.ap().rearrange("(c p) h -> p c h", p=128)
    wd_r = wdb_d.ap().rearrange("(c p) o -> p c o", p=128)

    k.wait("sy", "ve", ve_p2_done)
    k.wait("sy", "dG", 16 * 12)
    k.wait("gp", "ve", ve_p2_done)
    k.wait("gp", "dG", 16 * 12)

    pe_gu_use = {}
    se_gate = {}
    ve_hml = None
    ve_up = {}
    pe_wd_use = {}
    pe_down_tt = [None] * 4
    se_gtl = {}
    ve_comb = {}
    ve_comb_last = None
    out_n = 0
    npass = 2 if split else 1
    dF_mark = {}
    dD_mark = {}
    dD_hist = {}

    def dD_mark_at(h):
        return dD_hist[h]

    def emit_gu_load(h):
        if pe_gu_use.get(h % 2) is not None:
            k.wait("gp", "pe", pe_gu_use[h % 2])
        k.dma("gp", lambda h=h: wgb.ap()[:, h % 2],
              lambda h=h: wg_r[:, :, h * 128:(h + 1) * 128], f"dF{h % 2}")
        k.dma("gp", lambda h=h: wub.ap()[:, h % 2],
              lambda h=h: wu_r[:, :, h * 128:(h + 1) * 128], f"dF{h % 2}")
        dF_mark[h] = (f"dF{h % 2}", k.cnt[f"dF{h % 2}"])

    for s4 in range(NSLAB):
        if s4 == 0:
            emit_gu_load(0)
            emit_gu_load(1)
        for h in range(NH):
            dF_h = dF_mark[h]
            gbank = (h % 2) * 2
            ubank = (h % 2) * 2 + 1
            k.wait("te", dF_h[0], dF_h[1])
            if s4 == 0 and h == 0:
                k.wait("te", "se", se_exp[NT - 1])
            elif h <= 1 and se_gtl:
                for key in (0, 1):
                    if se_gtl.get(key) is not None:
                        k.wait("te", "se", se_gtl[key])
            if se_gate.get(gbank) is not None:
                k.wait("te", "se", se_gate[gbank])
            if ve_up.get(ubank) is not None:
                k.wait("te", "ve", ve_up[ubank])
            for wbuf, bank in ((wgb, gbank), (wub, ubank)):
                terms = []
                for kc in range(KD):
                    for (wa, xa) in passes(
                            [lambda kc, h=h, wbuf=wbuf: wbuf.ap()[:, h % 2, kc, :]],
                            [(lambda t: (lambda kc, s4=s4: t.ap()[:, kc, s4 * SLAB:(s4 + 1) * SLAB]))(t)
                             for t in xnT]):
                        terms.append(((lambda wa=wa, kc=kc: wa(kc)),
                                      (lambda xa=xa, kc=kc: xa(kc))))
                mm_group(lambda bank=bank: psA.ap()[:, bank * 512:(bank + 1) * 512], terms)
                if wbuf is wgb:
                    pe_gate = k.cnt["pe"]
            pe_up = k.cnt["pe"]
            pe_gu_use[h % 2] = pe_up
            k.wait("se", "pe", pe_gate)
            if ve_hml is not None:
                k.wait("se", "ve", ve_hml)
            k.op("se", lambda e, gbank=gbank: e.activation(
                hsc.ap(), psA.ap()[:, gbank * 512:(gbank + 1) * 512], AF.Silu,
                bias=zeroc.ap(), scale=C127))
            k.drain_inc("se", "se")
            se_gate[gbank] = k.cnt["se"]
            k.wait("ve", "se", se_gate[gbank])
            k.wait("ve", "pe", pe_up)
            if split:
                k.op("ve", lambda e, ubank=ubank: e.scalar_tensor_tensor(
                    hfs.ap(), psA.ap()[:, ubank * 512:(ubank + 1) * 512], SH,
                    hsc.ap(), ALU.mult, ALU.mult))
                k.op("ve", lambda e: e.drain())
                k.op("ve", lambda e, h=h: e.tensor_copy(hT[0].ap()[:, h, :], hfs.ap()))
                k.op("ve", lambda e: e.drain())
                k.op("ve", lambda e, h=h: e.tensor_tensor(
                    hT[1].ap()[:, h, :], hfs.ap(), hT[0].ap()[:, h, :], ALU.subtract))
            else:
                k.op("ve", lambda e, h=h, ubank=ubank: e.tensor_tensor(
                    hT[0].ap()[:, h, :], hsc.ap(), psA.ap()[:, ubank * 512:(ubank + 1) * 512],
                    ALU.mult))
            k.drain_inc("ve", "ve")
            ve_up[ubank] = k.cnt["ve"]
            ve_hml = k.cnt["ve"]
            if h + 2 < NH:
                emit_gu_load(h + 2)
            if h == NH - 2:
                for hh0 in (0, 1):
                    if pe_wd_use.get(hh0) is not None:
                        k.wait("gp", "pe", pe_wd_use[hh0])
                    k.dma("gp", lambda hh0=hh0: wdb.ap()[:, hh0],
                          lambda hh0=hh0: wd_r[:, hh0, :], f"dD{hh0}")
                    dD_hist[hh0] = (f"dD{hh0}", k.cnt[f"dD{hh0}"])
        ve_h_done = k.cnt["ve"]
        def psX_ap(tt, n2):
            if tt == 0:
                return psA.ap()[:, n2 * 512:(n2 + 1) * 512]
            if tt == 1:
                return psA.ap()[:, 1024 + n2 * 512: 1024 + (n2 + 1) * 512]
            if tt == 2:
                return psB.ap()[:, n2 * 512:(n2 + 1) * 512]
            return psC.ap()[:, n2 * 512:(n2 + 1) * 512]

        dl_tt = [None] * 4

        def emit_sx_load(tt):
            g = s4 * 4 + tt
            k.wait("sy", "dS", 16 * (store_pos[g] + 1))
            if ve_comb.get(tt % 2) is not None:
                k.wait("sy", "ve", ve_comb[tt % 2])
            k.dma("sy", lambda tt=tt, g=g: sin.ap()[:, tt % 2, :],
                  lambda g=g: sscr_d.ap()[g * 128:(g + 1) * 128, :], f"dL{tt % 2}")
            k.dma("sy", lambda tt=tt, g=g: xin.ap()[:, tt % 2, :],
                  lambda g=g: x_r[g], f"dL{tt % 2}")
            dl_tt[tt] = (f"dL{tt % 2}", k.cnt[f"dL{tt % 2}"])

        emit_sx_load(0)
        emit_sx_load(1)
        for key in (0, 2):
            if se_gate.get(key) is not None:
                k.wait("te", "se", se_gate[key])
        for key in (1, 3):
            if ve_up.get(key) is not None:
                k.wait("te", "ve", ve_up[key])
        if s4 == 0:
            k.wait("te", "ve", ve_p2_done)
            k.wait("te", last_psC_evac[0][0], last_psC_evac[0][1])
            k.wait("te", last_psC_evac[1][0], last_psC_evac[1][1])
        else:
            for key in (2, 3):
                if se_gtl.get(key) is not None:
                    k.wait("te", "se", se_gtl[key])
        k.wait("te", "ve", ve_h_done)
        if s4 + 1 < NSLAB:
            emit_gu_load(0)
            emit_gu_load(1)
        for h in range(NH):
            if h >= 2:
                if pe_wd_use.get(h % 2) is not None:
                    k.wait("gp", "pe", pe_wd_use[h % 2])
                k.dma("gp", lambda h=h: wdb.ap()[:, h % 2],
                      lambda h=h: wd_r[:, h, :], f"dD{h % 2}")
                dD_hist[h] = (f"dD{h % 2}", k.cnt[f"dD{h % 2}"])
            k.wait("te", dD_mark_at(h)[0], dD_mark_at(h)[1])
            for tt in range(4):
                hparts = [(lambda t: (lambda h, lt: t.ap()[:, h, lt * 128:(lt + 1) * 128]))(t)
                          for t in hT]
                for pi in range(npass):
                    for n2 in range(2):
                        ha = hparts[pi]
                        tt_done = (h == NH - 1 and pi == npass - 1 and n2 == 1)
                        k.op("te", lambda e, ha=ha, h=h, tt=tt, n2=n2, pi=pi,
                             first=(h == 0 and pi == 0),
                             last=(h == NH - 1 and pi == npass - 1): e.matmul(
                                 psX_ap(tt, n2),
                                 ha(h, tt),
                                 wdb.ap()[:, h % 2, n2 * 512:(n2 + 1) * 512],
                                 start=first, stop=last),
                             inc=("pe" if (tt_done or (tt == 3 and pi == npass - 1
                                                       and n2 == 1)) else None))
                        if tt_done:
                            pe_down_tt[tt] = k.cnt["pe"]
            pe_wd_use[h % 2] = k.cnt["pe"]
        pe_down = k.cnt["pe"]
        for tt in range(4):
            g = s4 * 4 + tt
            gb = gtl if tt % 2 == 0 else gtl2
            k.wait("se", "pe", pe_down_tt[tt])
            if s4 == 0 and tt == 1 and out_n == 1:
                k.wait("se", "dS", 16 * NT)
            if ve_comb.get(tt % 2) is not None:
                k.wait("se", "ve", ve_comb[tt % 2])
            c_dn = C127SQ / SH if split else C127SQ
            k.op("se", lambda e, tt=tt, gb=gb, c_dn=c_dn: e.activation(
                gb.ap()[:, 0:512], psX_ap(tt, 0), AF.Silu, bias=zeroc.ap(), scale=c_dn))
            k.op("se", lambda e, tt=tt, gb=gb, c_dn=c_dn: e.activation(
                gb.ap()[:, 512:1024], psX_ap(tt, 1), AF.Silu, bias=zeroc.ap(), scale=c_dn))
            k.drain_inc("se", "se")
            se_gtl[tt] = k.cnt["se"]
            k.wait("ve", "se", se_gtl[tt])
            k.wait("ve", dl_tt[tt][0], dl_tt[tt][1])
            if out_n >= 2:
                k.wait("ve", f"dO{out_n % 2}", 16 * (out_n // 2))
            k.op("ve", lambda e, tt=tt, gb=gb: e.tensor_tensor(
                otl.ap()[:, tt % 2, :], sin.ap()[:, tt % 2, :], gb.ap(), ALU.mult))
            k.op("ve", lambda e: e.drain())
            k.op("ve", lambda e, tt=tt: e.tensor_tensor(
                otl.ap()[:, tt % 2, :], otl.ap()[:, tt % 2, :], xin.ap()[:, tt % 2, :],
                ALU.add))
            k.drain_inc("ve", "ve")
            ve_comb[tt % 2] = k.cnt["ve"]
            ve_comb_last = k.cnt["ve"]
            k.wait("sy", "ve", ve_comb[tt % 2])
            k.dma("sy", lambda g=g, tt=tt: out_d.ap()[g * 128:(g + 1) * 128, :],
                  lambda tt=tt: otl.ap()[:, tt % 2, :], f"dO{out_n % 2}")
            out_n += 1
            if tt + 2 < 4:
                emit_sx_load(tt + 2)

    k.wait("sy", "dO0", k.cnt["dO0"])
    k.wait("sy", "dO1", k.cnt["dO1"])
    k.wait("sy", "dS", k.cnt["dS"])
    k.wait("sy", "dL0", k.cnt["dL0"])
    k.wait("sy", "dL1", k.cnt["dL1"])
    for s in ("dX0", "dX1", "dWq0", "dWq1", "dWv", "dWo", "dF0", "dF1", "dD0", "dD1"):
        k.wait("sy", s, k.cnt[s])
    k.wait("gp", "dG", k.cnt["dG"])

    import contextlib
    with contextlib.ExitStack() as stack:
        sem_objs = {s: stack.enter_context(nc.semaphore(s)) for s in k.cnt}
        block = stack.enter_context(nc.Block())

        def body(eng_name):
            def run(eng):
                for item in k.ops[eng_name]:
                    if item[0] == "w":
                        eng.wait_ge(sem_objs[item[1]], item[2])
                    elif item[0] == "d":
                        eng.drain().then_inc(sem_objs[item[1]], item[2])
                    else:
                        _, fn, inc, amt = item
                        ins = fn(eng)
                        if inc is not None:
                            ins.then_inc(sem_objs[inc], amt)
            return run
        block.sync(body("sy"))
        block.gpsimd(body("gp"))
        block.tensor(body("te"))
        block.vector(body("ve"))
        block.scalar(body("se"))
    return nc


_PROG_CACHE = {}


def kernel(L0_attn_x, L0_attn_w_qkv, L0_attn_w_out, L0_attn_b_out,
           L0_ffn_w_gate, L0_ffn_w_up, L0_ffn_w_down):
    x = np.ascontiguousarray(np.asarray(L0_attn_x, dtype=np.float32))
    wqkvT = np.ascontiguousarray(np.asarray(L0_attn_w_qkv, dtype=np.float32).T)
    woutT = np.ascontiguousarray(np.asarray(L0_attn_w_out, dtype=np.float32).T)
    brow = np.ascontiguousarray(np.asarray(L0_attn_b_out, dtype=np.float32).reshape(1, D))
    wgT = np.ascontiguousarray(np.asarray(L0_ffn_w_gate, dtype=np.int32).T)
    wuT = np.ascontiguousarray(np.asarray(L0_ffn_w_up, dtype=np.int32).T)
    wdT = np.ascontiguousarray(np.asarray(L0_ffn_w_down, dtype=np.int32).T)
    ident = np.eye(128, dtype=np.float32)
    mask = np.where(np.arange(128)[None, :] > np.arange(128)[:, None],
                    np.float32(NEG), np.float32(0.0)).astype(np.float32)

    if MODE not in _PROG_CACHE:
        _PROG_CACHE[MODE] = build_program(MODE)
    nc = _PROG_CACHE[MODE]

    in_maps = []
    for c in range(8):
        in_maps.append({
            "x": np.ascontiguousarray(x[c]),
            "wqkvT": wqkvT, "woutT": woutT, "brow": brow,
            "wgT": wgT, "wuT": wuT, "wdT": wdT,
            "ident": ident, "mask": mask,
        })
    res = run_bass_kernel_spmd(nc, in_maps, core_ids=list(range(8)))
    return np.stack([res.results[c]["out"] for c in range(8)], axis=0)


# version 34
# speedup vs baseline: 1.0247x; 1.0247x over previous
import math
import numpy as np

import concourse.bass as bass
import concourse.mybir as mybir
import concourse.bass_utils as _bu
from concourse.bass_utils import run_bass_kernel_spmd

if not getattr(_bu, "_ldw_patched", False):
    _orig_run_command = _bu.run_command

    def _run_command_ldw(argv, **kw):
        argv = ["--enable-ldw-opt=true" if a == "--enable-ldw-opt=false" else a
                for a in argv]
        return _orig_run_command(argv, **kw)

    _bu.run_command = _run_command_ldw
    _bu._ldw_patched = True

F32 = mybir.dt.float32
BF16 = mybir.dt.bfloat16
FP16 = mybir.dt.float16
I32 = mybir.dt.int32
AF = mybir.ActivationFunctionType
ALU = mybir.AluOpType
AX = mybir.AxisListType

MODE = "split"

T = 2048
D = 1024
DQ = 512
HID = 3072
NT = T // 128
KD = D // 128
KQ = DQ // 128
NH = HID // 128
SLAB = 512
NSLAB = T // SLAB
EPS = 1e-5
C_ATT = 1.0 / math.sqrt(DQ)
C127 = 1.0 / 127.0
C127SQ = 1.0 / (127.0 * 127.0)
NEG = -1.0e30
SA = 64.0
SP = 16.0
SG = 1.0 / 16.0
SD = 64.0
SH = 1.0 / 64.0

ENGINES = ("sy", "gp", "te", "ve", "se")


class Builder:

    def __init__(self):
        self.ops = {e: [] for e in ENGINES}
        self.cnt = {}

    def sem(self, name):
        self.cnt[name] = 0

    def wait(self, eng, sem, val):
        assert val <= self.cnt[sem], (eng, sem, val, self.cnt[sem])
        if val > 0:
            self.ops[eng].append(("w", sem, val))

    def op(self, eng, fn, inc=None, amt=1):
        self.ops[eng].append(("c", fn, inc, amt))
        if inc is not None:
            self.cnt[inc] += amt

    def drain_inc(self, eng, sem, amt=1):
        self.ops[eng].append(("d", sem, amt))
        self.cnt[sem] += amt

    def dma(self, eng, out_ap_fn, in_ap_fn, sem):
        def fn(e):
            return e.dma_start(out=out_ap_fn(), in_=in_ap_fn())
        self.op(eng, fn, inc=sem, amt=16)


def build_program(mode):
    split = mode == "split"
    nc = bass.Bass("TRN2", target_bir_lowering=False)

    x_d = nc.dram_tensor("x", [T, D], F32, kind="ExternalInput")
    wqkvT_d = nc.dram_tensor("wqkvT", [D, 3 * DQ], F32, kind="ExternalInput")
    woutT_d = nc.dram_tensor("woutT", [DQ, D], F32, kind="ExternalInput")
    brow_d = nc.dram_tensor("brow", [1, D], F32, kind="ExternalInput")
    wgT_d = nc.dram_tensor("wgT", [D, HID], I32, kind="ExternalInput")
    wuT_d = nc.dram_tensor("wuT", [D, HID], I32, kind="ExternalInput")
    wdT_d = nc.dram_tensor("wdT", [HID, D], I32, kind="ExternalInput")
    ident_d = nc.dram_tensor("ident", [128, 128], F32, kind="ExternalInput")
    mask_d = nc.dram_tensor("mask", [128, 128], F32, kind="ExternalInput")
    out_d = nc.dram_tensor("out", [T, D], F32, kind="ExternalOutput")
    sscr_d = nc.dram_tensor("sscr", [T, D], F32)
    WT = FP16 if split else F32
    wgb_d = nc.dram_tensor("wgb_scr", [D, HID], WT)
    wub_d = nc.dram_tensor("wub_scr", [D, HID], WT)
    wdb_d = nc.dram_tensor("wdb_scr", [HID, D], WT)

    SBUF_BASE = 16896
    off = [SBUF_BASE]

    def place(name, shape, dtype, at=None):
        nbytes = int(np.prod(shape[1:])) * mybir.dt.size(dtype)
        o = off[0] if at is None else at
        o = (o + 31) // 32 * 32
        t = nc.alloc_sbuf_tensor_at(name, list(shape), dtype, offset=o)
        end = o + nbytes
        off[0] = max(off[0], end)
        return t

    ident = place("identf", [128, 128], F32)
    maskt = place("maskt", [128, 128], F32)
    bbc_off = (off[0] + 31) // 32 * 32
    b_bc = place("b_bc", [128, D], F32)
    brow = place("brow_s", [1, D], F32, at=bbc_off)
    ones1 = place("ones1", [1, 128], F32)
    st12 = place("st12", [128, 2, 12], F32)
    mv = place("mv", [128, 2, 2], F32)
    stdv = place("stdv", [128, 2], F32)
    inv = place("inv", [128, 2], F32)
    negmi = place("negmi", [128, 2], F32)
    negmax = place("negmax", [128, 1], F32)
    negcm = place("negcm", [128, 1], F32)
    rsum = place("rsum", [128, 1], F32)
    invr = place("invr", [128, 1], F32)
    epsb = place("epsb", [128, 1], F32)
    zeroc = place("zeroc", [128, 1], F32)

    if split:
        xnT = (place("xnT_hi", [128, KD, T], FP16),
               place("xnT_lo", [128, KD, T], FP16))
    else:
        xnT = (place("xnT", [128, KD, T], F32),)

    r2_base = off[0]
    if split:
        qT = (place("qT_hi", [128, KQ, T], FP16), place("qT_lo", [128, KQ, T], FP16))
        kT = (place("kT_hi", [128, KQ, T], FP16), place("kT_lo", [128, KQ, T], FP16))
        vtk = (place("v_hi", [128, NT, DQ], FP16), place("v_lo", [128, NT, DQ], FP16))
    else:
        qT = (place("qT", [128, KQ, T], F32),)
        kT = (place("kT", [128, KQ, T], F32),)
        vtk = (place("vtk", [128, NT, DQ], F32),)
    r2_end_a = off[0]
    wvf = place("wvf", [128, KD, DQ], F32, at=r2_base + 32768)
    if split:
        wvh = place("wvh", [128, KD, DQ], FP16, at=r2_base + 49152)
        wvl = place("wvl", [128, KD, DQ], FP16, at=r2_base + 57344)
    off[0] = r2_base
    if split:
        hT = (place("hT_hi", [128, NH, SLAB], FP16, at=r2_base),
              place("hT_lo", [128, NH, SLAB], FP16))
        wgb = place("wgb", [128, 2, KD, 128], FP16)
        wub = place("wub", [128, 2, KD, 128], FP16)
        wdb = place("wdb", [128, 2, D], FP16)
    else:
        hT = (place("hT", [128, NH, SLAB], F32, at=r2_base),)
        wgb = place("wgb", [128, 2, KD, 128], F32)
        wub = place("wub", [128, 2, KD, 128], F32)
        wdb = place("wdb", [128, 2, D], F32)
    off[0] = max(off[0], r2_end_a)

    r3_base = off[0]
    r3 = r3_base

    def at(kb):
        return r3 + int(kb * 1024)

    xt = place("xt", [128, 2, D], F32, at=at(0))
    xnsc = place("xnsc", [128, 2, D], F32, at=at(8))
    pfull = place("pfull", [128, T], F32, at=at(0))
    if split:
        pT = (place("pT_hi", [128, NT, 128], FP16, at=at(8)),
              place("pT_lo", [128, NT, 128], FP16, at=at(12)))
    else:
        pT = (place("pT", [128, NT, 128], F32, at=at(8)),)

    wqf = place("wqf", [128, 2, KD, 128], F32, at=at(16))
    if split:
        wqh = place("wqh", [128, 2, KD, 128], FP16, at=at(24))
        wql = place("wql", [128, 2, KD, 128], FP16, at=at(28))
        wo = (place("wo_hi", [128, KQ, D], FP16, at=at(16)),
              place("wo_lo", [128, KQ, D], FP16, at=at(24)))
    else:
        wof = place("wof", [128, KQ, D], F32, at=at(16))
        wo = (wof,)

    if split:
        oaT = (place("oaT_hi", [128, KQ, 128], FP16, at=at(33)),
               place("oaT_lo", [128, KQ, 128], FP16, at=at(34)))
    else:
        oaT = (place("oaT", [128, KQ, 128], F32, at=at(33)),)
    oaf = place("oaf", [128, DQ], F32, at=at(35))
    stl = place("stl", [128, D], F32, at=at(37))
    if split:
        wof = place("wof_st", [128, 1, D], F32, at=at(37))
        wof2 = place("wof_st2", [128, 1, D], F32, at=at(33))
    sin = place("sin", [128, 2, D], F32, at=at(0))
    xin = place("xin", [128, 2, D], F32, at=at(8))
    otl = place("otl", [128, 2, D], F32, at=at(16))
    gtl = place("gtl", [128, D], F32, at=at(24))
    hsc = place("hsc", [128, SLAB], F32, at=at(28))
    if split:
        hfs = place("hfs", [128, SLAB], F32, at=at(30))
    gtl2 = place("gtl2", [128, D], F32, at=at(37))
    off[0] = max(off[0], at(41))
    assert off[0] <= 229200, f"SBUF over budget: {off[0]}"

    psA = nc.alloc_psum_tensor("psA", [128, 2048], F32)
    psB = nc.alloc_psum_tensor("psB", [128, 1024], F32)
    psC = nc.alloc_psum_tensor("psC", [128, 1024], F32)

    k = Builder()
    for s in ("dC", "dG", "dX0", "dX1", "dWq0", "dWq1", "dWv", "dWo", "vw", "dS", "dL0", "dL1", "dO0", "dO1",
              "dF0", "dF1", "dD0", "dD1", "pe", "ve", "se", "gq"):
        k.sem(s)


    k.dma("sy", lambda: ident.ap(), lambda: ident_d.ap(), "dC")
    k.dma("sy", lambda: maskt.ap(), lambda: mask_d.ap(), "dC")
    k.dma("sy", lambda: brow.ap(), lambda: brow_d.ap(), "dC")
    k.wait("ve", "dC", 48)
    k.op("ve", lambda e: e.memset(ones1.ap(), 1.0))
    k.op("ve", lambda e: e.memset(epsb.ap(), EPS))
    k.op("ve", lambda e: e.memset(zeroc.ap(), 0.0))
    k.drain_inc("ve", "ve")
    ve_ones = k.cnt["ve"]
    k.wait("te", "dC", 48)
    k.wait("te", "ve", ve_ones)
    for n in range(2):
        k.op("te", lambda e, n=n: e.matmul(
            psB.ap()[:, n * 512:(n + 1) * 512], ones1.ap(),
            brow.ap()[:, n * 512:(n + 1) * 512], start=True, stop=True),
            inc=("pe" if n == 1 else None))
    k.wait("ve", "pe", k.cnt["pe"])
    k.op("ve", lambda e: e.tensor_copy(b_bc.ap(), psB.ap()))
    k.drain_inc("ve", "ve")
    ve_bbc = k.cnt["ve"]

    def mm_group(psum_ap_fn, terms, inc="pe"):
        n = len(terms)
        for j, (lf, rf) in enumerate(terms):
            k.op("te", lambda e, lf=lf, rf=rf, j=j, n=n, pf=psum_ap_fn: e.matmul(
                pf(), lf(), rf(), start=(j == 0), stop=(j == n - 1)),
                inc=(inc if j == n - 1 else None))

    def passes(a_bufs, b_bufs):
        if len(a_bufs) == 1 and len(b_bufs) == 1:
            return [(a_bufs[0], b_bufs[0])]
        if len(a_bufs) == 1:
            return [(a_bufs[0], b_bufs[0]), (a_bufs[0], b_bufs[1])]
        if len(b_bufs) == 1:
            return [(a_bufs[0], b_bufs[0]), (a_bufs[1], b_bufs[0])]
        return [(a_bufs[0], b_bufs[0]), (a_bufs[0], b_bufs[1]), (a_bufs[1], b_bufs[0])]

    x_r = x_d.ap().rearrange("(i p) d -> i p d", p=128)
    wq_r = wqkvT_d.ap().rearrange("(c p) e -> p c e", p=128)
    se_lnapply = [None] * NT
    ve_xn_evac = [None] * NT
    pe_xn_tr = [None] * NT
    pe_xn_tr_half = [None] * NT
    ve_wsplit = [None] * 8
    te_wuse = [None] * 8
    ve_evac_qk = {}
    qk_rr = [0]

    for i in range(2):
        k.dma("sy", lambda i=i: xt.ap()[:, i % 2, :], lambda i=i: x_r[i], f"dX{i % 2}")
    k.dma("sy", lambda: wvf.ap(), lambda: wq_r[:, :, 1024:1536], "dWv")

    def emit_wq_load(mi):
        if mi >= 2:
            if split:
                k.wait("sy", "ve", ve_wsplit[mi - 2])
            else:
                k.wait("sy", "pe", te_wuse[mi - 2])
        k.dma("sy", lambda mi=mi: wqf.ap()[:, mi % 2],
              lambda mi=mi: wq_r[:, :, mi * 128:(mi + 1) * 128], f"dWq{mi % 2}")

    def emit_wsplit(mi):
        if split:
            k.wait("ve", f"dWq{mi % 2}", 16 * (mi // 2 + 1))
            if mi >= 2:
                k.wait("ve", "pe", te_wuse[mi - 2])
            k.op("ve", lambda e, mi=mi: e.tensor_scalar_mul(wqh.ap()[:, mi % 2],
                                                            wqf.ap()[:, mi % 2], SA))
            k.op("ve", lambda e: e.drain())
            k.op("ve", lambda e, mi=mi: e.scalar_tensor_tensor(
                wql.ap()[:, mi % 2], wqf.ap()[:, mi % 2], SA, wqh.ap()[:, mi % 2],
                ALU.mult, ALU.subtract))
            k.drain_inc("ve", "ve")
        ve_wsplit[mi] = k.cnt["ve"]

    def emit_wv_split(kc_lo, kc_hi):
        if not split:
            return
        k.wait("ve", "dWv", 16)
        k.op("ve", lambda e, kc_lo=kc_lo, kc_hi=kc_hi: e.tensor_scalar_mul(
            wvh.ap()[:, kc_lo:kc_hi], wvf.ap()[:, kc_lo:kc_hi], SA))
        k.op("ve", lambda e: e.drain())
        k.op("ve", lambda e, kc_lo=kc_lo, kc_hi=kc_hi: e.scalar_tensor_tensor(
            wvl.ap()[:, kc_lo:kc_hi], wvf.ap()[:, kc_lo:kc_hi], SA,
            wvh.ap()[:, kc_lo:kc_hi], ALU.mult, ALU.subtract))
        k.drain_inc("ve", "ve")

    ve_wv_split = [0]
    ve_evac_v = ve_evac_qk

    def emit_v_group(hh, tc, xn_ready):
        k.wait("te", xn_ready[0], xn_ready[1])
        if split:
            k.wait("te", "ve", ve_wv_split[0])
            wvb = [lambda kc: wvh.ap()[:, kc, hh * 256:(hh + 1) * 256],
                   lambda kc: wvl.ap()[:, kc, hh * 256:(hh + 1) * 256]]
        else:
            k.wait("te", "dWv", 16)
            wvb = [lambda kc: wvf.ap()[:, kc, hh * 256:(hh + 1) * 256]]
        bank = qk_rr[0] % 4
        qk_rr[0] += 1
        prev = ve_evac_qk.get(bank)
        if prev is not None:
            k.wait("te", prev[0], prev[1])
        terms = []
        for kc in range(KD):
            for (xb, wv) in passes([
                    (lambda t: (lambda kc, tc=tc: t.ap()[:, kc, tc * 128:(tc + 1) * 128]))(t)
                    for t in xnT], wvb):
                terms.append(((lambda xb=xb, kc=kc: xb(kc)),
                              (lambda wv=wv, kc=kc: wv(kc))))
        mm_group(lambda bank=bank: psA.ap()[:, bank * 512: bank * 512 + 256], terms)
        pe_g = k.cnt["pe"]
        k.wait("se", "pe", pe_g)
        k.op("se", lambda e, tc=tc, hh=hh, bank=bank: e.activation(
            vtk[0].ap()[:, tc, hh * 256:(hh + 1) * 256],
            psA.ap()[:, bank * 512: bank * 512 + 256], AF.Identity, bias=zeroc.ap()))
        k.drain_inc("se", "se")
        se_c = k.cnt["se"]
        if split:
            k.wait("ve", "se", se_c)
            k.op("ve", lambda e, tc=tc, hh=hh, bank=bank: e.tensor_tensor(
                vtk[1].ap()[:, tc, hh * 256:(hh + 1) * 256],
                psA.ap()[:, bank * 512: bank * 512 + 256],
                vtk[0].ap()[:, tc, hh * 256:(hh + 1) * 256], ALU.subtract))
            k.drain_inc("ve", "ve")
            ve_evac_qk[bank] = ("ve", k.cnt["ve"])
        else:
            ve_evac_qk[bank] = ("se", se_c)

    def emit_qk_group(mi, n, xn_ready):
        k.wait("te", xn_ready[0], xn_ready[1])
        if split:
            k.wait("te", "ve", ve_wsplit[mi])
            wbufs = [lambda kc, mi=mi: wqh.ap()[:, mi % 2, kc, :],
                     lambda kc, mi=mi: wql.ap()[:, mi % 2, kc, :]]
        else:
            k.wait("te", f"dWq{mi % 2}", 16 * (mi // 2 + 1))
            wbufs = [lambda kc, mi=mi: wqf.ap()[:, mi % 2, kc, :]]
        bank = qk_rr[0] % 4
        qk_rr[0] += 1
        prev = ve_evac_qk.get(bank)
        if prev is not None:
            k.wait("te", prev[0], prev[1])
        terms = []
        for kc in range(KD):
            for (wa, xb) in passes(wbufs, [
                    (lambda t: (lambda kc, n: t.ap()[:, kc, n * 512:(n + 1) * 512]))(t)
                    for t in xnT]):
                terms.append(((lambda wa=wa, kc=kc: wa(kc)),
                              (lambda xb=xb, kc=kc, n=n: xb(kc, n))))
        mm_group(lambda bank=bank: psA.ap()[:, bank * 512:(bank + 1) * 512], terms)
        pe_g = k.cnt["pe"]
        dst = qT if mi < 4 else kT
        m = mi % 4
        k.wait("se", "pe", pe_g)
        k.op("se", lambda e, dst=dst, m=m, n=n, bank=bank: e.activation(
            dst[0].ap()[:, m, n * 512:(n + 1) * 512],
            psA.ap()[:, bank * 512:(bank + 1) * 512], AF.Identity, bias=zeroc.ap()))
        k.drain_inc("se", "se")
        se_c = k.cnt["se"]
        if split:
            k.wait("ve", "se", se_c)
            k.op("ve", lambda e, dst=dst, m=m, n=n, bank=bank: e.tensor_tensor(
                dst[1].ap()[:, m, n * 512:(n + 1) * 512],
                psA.ap()[:, bank * 512:(bank + 1) * 512],
                dst[0].ap()[:, m, n * 512:(n + 1) * 512], ALU.subtract))
            k.drain_inc("ve", "ve")
            ve_evac_qk[bank] = ("ve", k.cnt["ve"])
        else:
            ve_evac_qk[bank] = ("se", se_c)

    emit_wq_load(0)
    emit_wq_load(1)

    def emit_ln_chain(j):
        p = j % 2
        k.wait("ve", f"dX{p}", 16 * (j // 2 + 1))
        if j >= 2:
            k.wait("ve", "se", se_lnapply[j - 2])
        k.op("ve", lambda e, p=p: e.bn_stats(st12.ap()[:, p, 0:6], xt.ap()[:, p, 0:512]))
        k.op("ve", lambda e, p=p: e.bn_stats(st12.ap()[:, p, 6:12], xt.ap()[:, p, 512:1024]))
        k.op("ve", lambda e: e.drain())
        k.op("ve", lambda e, p=p: e.bn_aggr(mv.ap()[:, p, :], st12.ap()[:, p, :]))
        k.drain_inc("ve", "ve")
        ve_mv = k.cnt["ve"]
        k.wait("se", "ve", ve_mv)
        k.op("se", lambda e, p=p: e.activation(stdv.ap()[:, p:p + 1], mv.ap()[:, p, 1:2],
                                               AF.Sqrt, bias=epsb.ap()))
        k.drain_inc("se", "se")
        se_std = k.cnt["se"]
        k.wait("ve", "se", se_std)
        k.op("ve", lambda e, p=p: e.reciprocal(inv.ap()[:, p:p + 1], stdv.ap()[:, p:p + 1]))
        k.op("ve", lambda e: e.drain())
        k.op("ve", lambda e, p=p: e.tensor_scalar(negmi.ap()[:, p:p + 1], mv.ap()[:, p, 0:1],
                                                  inv.ap()[:, p:p + 1], -1.0,
                                                  ALU.mult, ALU.mult))
        k.drain_inc("ve", "ve")
        ve_ln = k.cnt["ve"]
        k.wait("se", "ve", ve_ln)
        if j >= 2:
            k.wait("se", "pe", pe_xn_tr[j - 2])
        k.op("se", lambda e, p=p: e.activation(xnsc.ap()[:, p, :], xt.ap()[:, p, :],
                                               AF.Identity, bias=negmi.ap()[:, p:p + 1],
                                               scale=inv.ap()[:, p:p + 1]))
        k.drain_inc("se", "se")
        se_lnapply[j] = k.cnt["se"]
        if j + 2 < NT:
            k.wait("sy", "se", se_lnapply[j])
            k.dma("sy", lambda j=j: xt.ap()[:, j % 2, :], lambda j=j: x_r[j + 2], f"dX{j % 2}")

    def emit_xn_evac(j):
        pe_need = pe_xn_tr_half[j]
        ps = psC if j % 2 == 0 else psB
        for half in range(2):
            k.wait("se", "pe", pe_need[half])
            src = ps.ap().rearrange("p (h c n) -> p h c n", h=2, c=4)[:, half]
            lo_s = half * 4
            k.op("se", lambda e, j=j, lo_s=lo_s, src=src: e.activation(
                xnT[0].ap()[:, lo_s:lo_s + 4, j * 128:(j + 1) * 128], src,
                AF.Identity, bias=zeroc.ap()))
            k.drain_inc("se", "se")
            se_c = k.cnt["se"]
            if split:
                k.wait("ve", "se", se_c)
                k.op("ve", lambda e, j=j, lo_s=lo_s, src=src: e.tensor_tensor(
                    xnT[1].ap()[:, lo_s:lo_s + 4, j * 128:(j + 1) * 128], src,
                    xnT[0].ap()[:, lo_s:lo_s + 4, j * 128:(j + 1) * 128], ALU.subtract))
                k.drain_inc("ve", "ve")
        ve_xn_evac[j] = ("ve", k.cnt["ve"]) if split else ("se", k.cnt["se"])

    emit_ln_chain(0)
    emit_ln_chain(1)
    emit_wv_split(0, 4)
    emit_wv_split(4, 8)
    ve_wv_split[0] = k.cnt["ve"]
    for i in range(NT):
        k.wait("te", "se", se_lnapply[i])
        ps = psC if i % 2 == 0 else psB
        if i == 1:
            k.wait("te", "ve", ve_bbc)
        pe_need = [None, None]
        for half in range(2):
            if i >= 2:
                k.wait("te", ve_xn_evac[i - 2][0], ve_xn_evac[i - 2][1])
            for c4 in range(4):
                c = half * 4 + c4
                k.op("te", lambda e, half=half, c4=c4, c=c, p=i % 2, ps=ps: e.transpose(
                    ps.ap()[:, half * 512 + c4 * 128: half * 512 + (c4 + 1) * 128],
                    xnsc.ap()[:, p, c * 128:(c + 1) * 128], ident.ap()),
                    inc=("pe" if c4 == 3 else None))
            pe_need[half] = k.cnt["pe"]
        pe_xn_tr_half[i] = pe_need
        pe_xn_tr[i] = k.cnt["pe"]
        if i >= 1 and (i - 1) % 4 == 3:
            n = (i - 1) // 4
            emit_qk_group(0, n, ve_xn_evac[i - 1])
            emit_qk_group(1, n, ve_xn_evac[i - 1])
        if i == 2:
            emit_v_group(0, 0, ve_xn_evac[0])
            emit_v_group(1, 0, ve_xn_evac[0])
        if 2 <= i <= 15:
            tc = i - 1
            emit_v_group(0, tc, ve_xn_evac[tc])
            emit_v_group(1, tc, ve_xn_evac[tc])
        emit_xn_evac(i)
        if i + 2 < NT:
            emit_ln_chain(i + 2)
        if i == 1:
            emit_wsplit(0)
            emit_wsplit(1)
    emit_qk_group(0, 3, ve_xn_evac[NT - 1])
    emit_qk_group(1, 3, ve_xn_evac[NT - 1])
    emit_v_group(0, NT - 1, ve_xn_evac[NT - 1])
    emit_v_group(1, NT - 1, ve_xn_evac[NT - 1])
    te_wuse[0] = te_wuse[1] = k.cnt["pe"]
    ve_xn_done = k.cnt["ve"]


    emit_wq_load(2)
    emit_wq_load(3)
    xn_all = ve_xn_evac[NT - 1]
    emit_wsplit(2)
    for mi in range(2, 8):
        if mi + 1 < 8:
            emit_wsplit(mi + 1)
        for n in range(4):
            emit_qk_group(mi, n, xn_all)
        te_wuse[mi] = k.cnt["pe"]
        if mi + 2 < 8:
            emit_wq_load(mi + 2)
    ve_p1_done = k.cnt["ve"]
    se_p1_done = k.cnt["se"]

    se_exp = [None] * NT
    pe_ptr_done = [None] * NT
    pe_scores_l = [None] * NT
    ve_sevac_cnt = None
    last_psC_evac = {0: ("ve", k.cnt["ve"]), 1: ("ve", k.cnt["ve"])}
    psb5_free = [None]
    ve_sevac_b4 = None

    def emit_scores(i):
        ncols = (i + 1) * 128
        nblk = (ncols + 511) // 512
        if prv.get(i) is None:
            k.wait("te", "ve", ve_p1_done)
            k.wait("te", "se", se_p1_done)
        else:
            k.wait("te", "se", se_exp[prv[i]])
        for b in range(nblk):
            nb = min(512, ncols - b * 512)
            terms = []
            for kc in range(KQ):
                for (qa, ka) in passes(
                        [(lambda t: (lambda kc, i=i: t.ap()[:, kc, i * 128:(i + 1) * 128]))(t)
                         for t in qT],
                        [(lambda t: (lambda kc, b=b, nb=nb: t.ap()[:, kc, b * 512: b * 512 + nb]))(t)
                         for t in kT]):
                    terms.append(((lambda qa=qa, kc=kc: qa(kc)),
                                  (lambda ka=ka, kc=kc: ka(kc))))
            mm_group(lambda b=b, nb=nb: psA.ap()[:, b * 512: b * 512 + nb], terms)
        pe_scores_l[i] = k.cnt["pe"]

    def softprep_exp(j):
        ncols = (j + 1) * 128
        k.wait("ve", "pe", pe_scores_l[j])
        k.op("ve", lambda e, j=j: e.tensor_tensor(
            psA.ap()[:, j * 128:(j + 1) * 128], psA.ap()[:, j * 128:(j + 1) * 128],
            maskt.ap(), ALU.add))
        k.op("ve", lambda e: e.drain())
        k.op("ve", lambda e, ncols=ncols: e.reduce_max(
            negmax.ap(), psA.ap()[:, 0:ncols], axis=AX.X, negate=True))
        k.op("ve", lambda e: e.drain())
        k.op("ve", lambda e: e.tensor_scalar_mul(negcm.ap(), negmax.ap(),
                                                 C_ATT / (SA * SA) if split else C_ATT))
        k.drain_inc("ve", "ve")
        ve_sm = k.cnt["ve"]
        k.wait("se", "ve", ve_sm)
        if prv.get(j) is not None:
            k.wait("se", "pe", pe_ptr_done[prv[j]])
        k.op("se", lambda e, ncols=ncols: e.activation(
            pfull.ap()[:, 0:ncols], psA.ap()[:, 0:ncols], AF.Exp,
            bias=negcm.ap(), scale=(C_ATT / (SA * SA) if split else C_ATT),
            accum_out=rsum.ap()))
        k.drain_inc("se", "se")
        se_exp[j] = k.cnt["se"]

    tile_order = []
    for z in range(NT // 2):
        tile_order += [z, NT - 1 - z]
    store_pos = {g: idx for idx, g in enumerate(tile_order)}
    nxt = {tile_order[z]: tile_order[z + 1] for z in range(NT - 1)}
    prv = {tile_order[z + 1]: tile_order[z] for z in range(NT - 1)}

    emit_scores(tile_order[0])
    softprep_exp(tile_order[0])

    wo_r = woutT_d.ap().rearrange("(c p) o -> p c o", p=128)
    if split:
        stgs = [wof, wof2]
        for hw in range(KQ):
            if hw >= 2:
                k.wait("se", "vw", hw - 1)
            k.dma("se", lambda hw=hw: stgs[hw % 2].ap(),
                  lambda hw=hw: wo_r[:, hw: hw + 1, :], "dWo")
            k.wait("se", "dWo", k.cnt["dWo"])
            k.wait("se", "ve", ve_wsplit[7])
            k.op("se", lambda e, hw=hw: e.activation(
                wo[0].ap()[:, hw: hw + 1, :], stgs[hw % 2].ap(), AF.Identity,
                bias=zeroc.ap(), scale=SD))
            k.drain_inc("se", "se")
            se_c = k.cnt["se"]
            k.wait("ve", "se", se_c)
            k.wait("ve", "pe", te_wuse[7])
            k.op("ve", lambda e, hw=hw: e.scalar_tensor_tensor(
                wo[1].ap()[:, hw: hw + 1, :], stgs[hw % 2].ap(), SD,
                wo[0].ap()[:, hw: hw + 1, :], ALU.mult, ALU.subtract))
            k.drain_inc("ve", "vw")
        dW_wo = k.cnt["dWo"]
    else:
        k.wait("se", "pe", te_wuse[7])
        k.dma("se", lambda: wof.ap(), lambda: wo_r, "dWo")
        dW_wo = k.cnt["dWo"]
    vw_done = k.cnt["vw"]

    for i in tile_order:
        ncols = (i + 1) * 128
        k.wait("ve", "se", se_exp[i])
        k.op("ve", lambda e: e.reciprocal(invr.ap(), rsum.ap()))
        if split:
            k.op("ve", lambda e: e.drain())
            k.op("ve", lambda e: e.tensor_scalar_mul(
                invr.ap(), invr.ap(), 1.0 / (SA * SP * SG * SD)))
        k.drain_inc("ve", "ve")
        if nxt.get(i) is not None:
            emit_scores(nxt[i])
        k.wait("te", "se", se_exp[i])
        pT_ready = []
        tr_slots = [psC.ap()[:, 0:512], psC.ap()[:, 512:1024], psB.ap()[:, 512:1024]]
        tr_free = [last_psC_evac[0], last_psC_evac[1], psb5_free[0]]
        for g in range((i + 4) // 4):
            sl = g % 3
            if tr_free[sl] is not None:
                k.wait("te", tr_free[sl][0], tr_free[sl][1])
            if sl == 2 and ve_sevac_cnt is not None:
                k.wait("te", "ve", ve_sevac_cnt)
            jmax = min(4, i + 1 - g * 4)
            for j4 in range(jmax):
                j = g * 4 + j4
                k.op("te", lambda e, sl=sl, j4=j4, j=j: e.transpose(
                    tr_slots[sl][:, j4 * 128:(j4 + 1) * 128],
                    pfull.ap()[:, j * 128:(j + 1) * 128], ident.ap()),
                    inc=("pe" if j4 == jmax - 1 else None))
            pe_tr = k.cnt["pe"]
            src = tr_slots[sl].rearrange("p (c n) -> p c n", c=4)[:, 0:jmax]
            g4 = g * 4
            k.wait("se", "pe", pe_tr)
            k.op("se", lambda e, g4=g4, jmax=jmax, src=src: e.activation(
                pT[0].ap()[:, g4: g4 + jmax, :], src, AF.Identity, bias=zeroc.ap(),
                scale=(SP if split else 1.0)))
            k.drain_inc("se", "se")
            se_c = k.cnt["se"]
            if split:
                k.wait("ve", "se", se_c)
                k.op("ve", lambda e, g4=g4, jmax=jmax, src=src: e.scalar_tensor_tensor(
                    pT[1].ap()[:, g4: g4 + jmax, :], src, SP,
                    pT[0].ap()[:, g4: g4 + jmax, :], ALU.mult, ALU.subtract))
                k.drain_inc("ve", "ve")
                done = ("ve", k.cnt["ve"])
            else:
                done = ("se", se_c)
            tr_free[sl] = done
            if sl < 2:
                last_psC_evac[sl] = done
            else:
                psb5_free[0] = done
            pT_ready.append(done)
        pe_ptr_done[i] = k.cnt["pe"]
        if ve_sevac_cnt is not None:
            k.wait("te", "ve", ve_sevac_b4)
        else:
            k.wait("te", ve_xn_evac[NT - 1][0], ve_xn_evac[NT - 1][1])
        nmm = (i + 1) * (3 if split else 1)
        emitted = 0
        for j in range(i + 1):
            if j % 4 == 0:
                rdy = pT_ready[j // 4]
                k.wait("te", rdy[0], rdy[1])
            for (pa, va) in passes(
                    [(lambda t: (lambda j: t.ap()[:, j, :]))(t) for t in pT],
                    [(lambda t: (lambda j: t.ap()[:, j, :]))(t) for t in vtk]):
                k.op("te", lambda e, pa=pa, va=va, j=j,
                     first=(emitted == 0), last=(emitted == nmm - 1): e.matmul(
                         psB.ap()[:, 0:512], pa(j), va(j), start=first, stop=last),
                     inc=("pe" if emitted == nmm - 1 else None))
                emitted += 1
        pe_av = k.cnt["pe"]
        k.wait("se", "pe", pe_av)
        k.op("se", lambda e: e.activation(oaf.ap(), psB.ap()[:, 0:512], AF.Identity,
                                          bias=zeroc.ap()))
        k.drain_inc("se", "se")
        se_oaf = k.cnt["se"]
        k.wait("te", "se", se_oaf)
        oat_slots = [psC.ap()[:, 0:512], psC.ap()[:, 512:1024], psB.ap()[:, 512:1024]]
        slot_free = [last_psC_evac[0], last_psC_evac[1], psb5_free[0]]
        slot_evac = [None] * 3
        oa_evac_m = [None] * KQ
        for m in range(KQ):
            sl = m % 3
            if slot_free[sl] is not None:
                k.wait("te", slot_free[sl][0], slot_free[sl][1])
            k.op("te", lambda e, m=m, sl=sl: e.transpose(
                oat_slots[sl][:, 0:128],
                oaf.ap()[:, m * 128:(m + 1) * 128], ident.ap()), inc="pe")
            pe_oat = k.cnt["pe"]
            k.wait("se", "pe", pe_oat)
            k.op("se", lambda e, m=m, sl=sl: e.activation(
                oaT[0].ap()[:, m, :], oat_slots[sl][:, 0:128],
                AF.Identity, bias=zeroc.ap(), scale=(SG if split else 1.0)))
            k.drain_inc("se", "se")
            se_c = k.cnt["se"]
            if split:
                k.wait("ve", "se", se_c)
                k.op("ve", lambda e, m=m, sl=sl: e.scalar_tensor_tensor(
                    oaT[1].ap()[:, m, :], oat_slots[sl][:, 0:128], SG,
                    oaT[0].ap()[:, m, :], ALU.mult, ALU.subtract))
                k.drain_inc("ve", "ve")
                slot_evac[sl] = ("ve", k.cnt["ve"])
            else:
                slot_evac[sl] = ("se", se_c)
            slot_free[sl] = slot_evac[sl]
            oa_evac_m[m] = slot_evac[sl]
        last_psC_evac[0] = slot_evac[0]
        last_psC_evac[1] = slot_evac[1]
        if slot_evac[2] is not None:
            psb5_free[0] = slot_evac[2]
        if nxt.get(i) is not None:
            softprep_exp(nxt[i])
        if ve_sevac_cnt is None:
            if split:
                k.wait("te", "vw", vw_done)
            else:
                k.wait("te", "dWo", dW_wo)
        npass_o = 3 if split else 1
        for n2 in range(2):
            emitted = 0
            nmm = KQ * npass_o
            for kc in range(KQ):
                if n2 == 0:
                    rdy = oa_evac_m[kc]
                    k.wait("te", rdy[0], rdy[1])
                elif kc == 0:
                    k.wait("te", oa_evac_m[2][0], oa_evac_m[2][1])
                for (oa, wa) in passes(
                        [(lambda t: (lambda kc: t.ap()[:, kc, :]))(t) for t in oaT],
                        [(lambda t: (lambda kc, n2=n2: t.ap()[:, kc, n2 * 512:(n2 + 1) * 512]))(t)
                         for t in wo]):
                    k.op("te", lambda e, oa=oa, wa=wa, kc=kc, n2=n2,
                         first=(emitted == 0), last=(emitted == nmm - 1): e.matmul(
                             psB.ap()[:, n2 * 512:(n2 + 1) * 512], oa(kc), wa(kc),
                             start=first, stop=last),
                         inc=("pe" if emitted == nmm - 1 else None))
                    emitted += 1
        pe_s = k.cnt["pe"]
        k.wait("ve", "pe", pe_s)
        if store_pos[i] >= 1:
            k.wait("ve", "dS", 16 * store_pos[i])
        k.op("ve", lambda e: e.scalar_tensor_tensor(
            stl.ap()[:, 0:512], psB.ap()[:, 0:512], invr.ap(), b_bc.ap()[:, 0:512],
            ALU.mult, ALU.add))
        k.drain_inc("ve", "ve")
        ve_sevac_b4 = k.cnt["ve"]
        k.op("ve", lambda e: e.scalar_tensor_tensor(
            stl.ap()[:, 512:1024], psB.ap()[:, 512:1024], invr.ap(),
            b_bc.ap()[:, 512:1024], ALU.mult, ALU.add))
        k.drain_inc("ve", "ve")
        ve_sevac_cnt = k.cnt["ve"]
        k.wait("sy", "ve", ve_sevac_cnt)
        k.dma("sy", lambda i=i: sscr_d.ap()[i * 128:(i + 1) * 128, :], lambda: stl.ap(), "dS")
        sp = store_pos[i]
        if sp < 12:
            k.wait("gp", "dS", 16 * (sp + 1))
            if sp < 4:
                k.dma("gp", lambda sp=sp: wgb_d.ap()[sp * 256:(sp + 1) * 256, :],
                      lambda sp=sp: wgT_d.ap()[sp * 256:(sp + 1) * 256, :], "dG")
            elif sp < 8:
                j = sp - 4
                k.dma("gp", lambda j=j: wub_d.ap()[j * 256:(j + 1) * 256, :],
                      lambda j=j: wuT_d.ap()[j * 256:(j + 1) * 256, :], "dG")
            else:
                j = sp - 8
                k.dma("gp", lambda j=j: wdb_d.ap()[j * 768:(j + 1) * 768, :],
                      lambda j=j: wdT_d.ap()[j * 768:(j + 1) * 768, :], "dG")
    ve_p2_done = ve_sevac_cnt

    wg_r = wgb_d.ap().rearrange("(c p) h -> p c h", p=128)
    wu_r = wub_d.ap().rearrange("(c p) h -> p c h", p=128)
    wd_r = wdb_d.ap().rearrange("(c p) o -> p c o", p=128)

    k.wait("sy", "ve", ve_p2_done)
    k.wait("sy", "dG", 16 * 12)

    pe_gu_use = {}
    se_gate = {}
    ve_hml = None
    ve_up = {}
    pe_wd_use = {}
    pe_down_tt = [None] * 4
    se_gtl = {}
    ve_comb = {}
    ve_comb_last = None
    out_n = 0
    npass = 2 if split else 1
    dF_mark = {}
    dD_mark = {}
    dD_hist = {}

    def dD_mark_at(h):
        return dD_hist[h]

    def emit_gu_load(h):
        if pe_gu_use.get(h % 2) is not None:
            k.wait("sy", "pe", pe_gu_use[h % 2])
        k.dma("sy", lambda h=h: wgb.ap()[:, h % 2],
              lambda h=h: wg_r[:, :, h * 128:(h + 1) * 128], f"dF{h % 2}")
        k.dma("sy", lambda h=h: wub.ap()[:, h % 2],
              lambda h=h: wu_r[:, :, h * 128:(h + 1) * 128], f"dF{h % 2}")
        dF_mark[h] = (f"dF{h % 2}", k.cnt[f"dF{h % 2}"])

    for s4 in range(NSLAB):
        if s4 == 0:
            emit_gu_load(0)
            emit_gu_load(1)
        for h in range(NH):
            dF_h = dF_mark[h]
            gbank = (h % 2) * 2
            ubank = (h % 2) * 2 + 1
            k.wait("te", dF_h[0], dF_h[1])
            if s4 == 0 and h == 0:
                k.wait("te", "se", se_exp[NT - 1])
            elif h <= 1 and se_gtl:
                for key in (0, 1):
                    if se_gtl.get(key) is not None:
                        k.wait("te", "se", se_gtl[key])
            if se_gate.get(gbank) is not None:
                k.wait("te", "se", se_gate[gbank])
            if ve_up.get(ubank) is not None:
                k.wait("te", "ve", ve_up[ubank])
            for wbuf, bank in ((wgb, gbank), (wub, ubank)):
                terms = []
                for kc in range(KD):
                    for (wa, xa) in passes(
                            [lambda kc, h=h, wbuf=wbuf: wbuf.ap()[:, h % 2, kc, :]],
                            [(lambda t: (lambda kc, s4=s4: t.ap()[:, kc, s4 * SLAB:(s4 + 1) * SLAB]))(t)
                             for t in xnT]):
                        terms.append(((lambda wa=wa, kc=kc: wa(kc)),
                                      (lambda xa=xa, kc=kc: xa(kc))))
                mm_group(lambda bank=bank: psA.ap()[:, bank * 512:(bank + 1) * 512], terms)
                if wbuf is wgb:
                    pe_gate = k.cnt["pe"]
            pe_up = k.cnt["pe"]
            pe_gu_use[h % 2] = pe_up
            k.wait("se", "pe", pe_gate)
            if ve_hml is not None:
                k.wait("se", "ve", ve_hml)
            k.op("se", lambda e, gbank=gbank: e.activation(
                hsc.ap(), psA.ap()[:, gbank * 512:(gbank + 1) * 512], AF.Silu,
                bias=zeroc.ap(), scale=C127))
            k.drain_inc("se", "se")
            se_gate[gbank] = k.cnt["se"]
            k.wait("ve", "se", se_gate[gbank])
            k.wait("ve", "pe", pe_up)
            if split:
                k.op("ve", lambda e, ubank=ubank: e.scalar_tensor_tensor(
                    hfs.ap(), psA.ap()[:, ubank * 512:(ubank + 1) * 512], SH,
                    hsc.ap(), ALU.mult, ALU.mult))
                k.op("ve", lambda e: e.drain())
                k.op("ve", lambda e, h=h: e.tensor_copy(hT[0].ap()[:, h, :], hfs.ap()))
                k.op("ve", lambda e: e.drain())
                k.op("ve", lambda e, h=h: e.tensor_tensor(
                    hT[1].ap()[:, h, :], hfs.ap(), hT[0].ap()[:, h, :], ALU.subtract))
            else:
                k.op("ve", lambda e, h=h, ubank=ubank: e.tensor_tensor(
                    hT[0].ap()[:, h, :], hsc.ap(), psA.ap()[:, ubank * 512:(ubank + 1) * 512],
                    ALU.mult))
            k.drain_inc("ve", "ve")
            ve_up[ubank] = k.cnt["ve"]
            ve_hml = k.cnt["ve"]
            if h + 2 < NH:
                emit_gu_load(h + 2)
            if h == NH - 2:
                for hh0 in (0, 1):
                    if pe_wd_use.get(hh0) is not None:
                        k.wait("sy", "pe", pe_wd_use[hh0])
                    k.dma("sy", lambda hh0=hh0: wdb.ap()[:, hh0],
                          lambda hh0=hh0: wd_r[:, hh0, :], f"dD{hh0}")
                    dD_hist[hh0] = (f"dD{hh0}", k.cnt[f"dD{hh0}"])
        ve_h_done = k.cnt["ve"]
        def psX_ap(tt, n2):
            if tt == 0:
                return psA.ap()[:, n2 * 512:(n2 + 1) * 512]
            if tt == 1:
                return psA.ap()[:, 1024 + n2 * 512: 1024 + (n2 + 1) * 512]
            if tt == 2:
                return psB.ap()[:, n2 * 512:(n2 + 1) * 512]
            return psC.ap()[:, n2 * 512:(n2 + 1) * 512]

        dl_tt = [None] * 4

        def emit_sx_load(tt):
            g = s4 * 4 + tt
            k.wait("sy", "dS", 16 * (store_pos[g] + 1))
            if ve_comb.get(tt % 2) is not None:
                k.wait("sy", "gq", ve_comb[tt % 2])
            k.dma("sy", lambda tt=tt, g=g: sin.ap()[:, tt % 2, :],
                  lambda g=g: sscr_d.ap()[g * 128:(g + 1) * 128, :], f"dL{tt % 2}")
            k.dma("sy", lambda tt=tt, g=g: xin.ap()[:, tt % 2, :],
                  lambda g=g: x_r[g], f"dL{tt % 2}")
            dl_tt[tt] = (f"dL{tt % 2}", k.cnt[f"dL{tt % 2}"])

        emit_sx_load(0)
        emit_sx_load(1)
        for key in (0, 2):
            if se_gate.get(key) is not None:
                k.wait("te", "se", se_gate[key])
        for key in (1, 3):
            if ve_up.get(key) is not None:
                k.wait("te", "ve", ve_up[key])
        if s4 == 0:
            k.wait("te", "ve", ve_p2_done)
            k.wait("te", last_psC_evac[0][0], last_psC_evac[0][1])
            k.wait("te", last_psC_evac[1][0], last_psC_evac[1][1])
        else:
            for key in (2, 3):
                if se_gtl.get(key) is not None:
                    k.wait("te", "se", se_gtl[key])
        k.wait("te", "ve", ve_h_done)
        if s4 + 1 < NSLAB:
            emit_gu_load(0)
            emit_gu_load(1)
        for h in range(NH):
            if h >= 2:
                if pe_wd_use.get(h % 2) is not None:
                    k.wait("sy", "pe", pe_wd_use[h % 2])
                k.dma("sy", lambda h=h: wdb.ap()[:, h % 2],
                      lambda h=h: wd_r[:, h, :], f"dD{h % 2}")
                dD_hist[h] = (f"dD{h % 2}", k.cnt[f"dD{h % 2}"])
            k.wait("te", dD_mark_at(h)[0], dD_mark_at(h)[1])
            for tt in range(4):
                hparts = [(lambda t: (lambda h, lt: t.ap()[:, h, lt * 128:(lt + 1) * 128]))(t)
                          for t in hT]
                for pi in range(npass):
                    for n2 in range(2):
                        ha = hparts[pi]
                        tt_done = (h == NH - 1 and pi == npass - 1 and n2 == 1)
                        k.op("te", lambda e, ha=ha, h=h, tt=tt, n2=n2, pi=pi,
                             first=(h == 0 and pi == 0),
                             last=(h == NH - 1 and pi == npass - 1): e.matmul(
                                 psX_ap(tt, n2),
                                 ha(h, tt),
                                 wdb.ap()[:, h % 2, n2 * 512:(n2 + 1) * 512],
                                 start=first, stop=last),
                             inc=("pe" if (tt_done or (tt == 3 and pi == npass - 1
                                                       and n2 == 1)) else None))
                        if tt_done:
                            pe_down_tt[tt] = k.cnt["pe"]
            pe_wd_use[h % 2] = k.cnt["pe"]
        pe_down = k.cnt["pe"]
        for tt in range(4):
            g = s4 * 4 + tt
            gb = gtl if tt % 2 == 0 else gtl2
            k.wait("se", "pe", pe_down_tt[tt])
            if s4 == 0 and tt == 1 and out_n == 1:
                k.wait("se", "dS", 16 * NT)
            if ve_comb.get(tt % 2) is not None:
                k.wait("se", "gq", ve_comb[tt % 2])
            c_dn = C127SQ / SH if split else C127SQ
            k.op("se", lambda e, tt=tt, gb=gb, c_dn=c_dn: e.activation(
                gb.ap()[:, 0:512], psX_ap(tt, 0), AF.Silu, bias=zeroc.ap(), scale=c_dn))
            k.op("se", lambda e, tt=tt, gb=gb, c_dn=c_dn: e.activation(
                gb.ap()[:, 512:1024], psX_ap(tt, 1), AF.Silu, bias=zeroc.ap(), scale=c_dn))
            k.drain_inc("se", "se")
            se_gtl[tt] = k.cnt["se"]
            k.wait("gp", "se", se_gtl[tt])
            k.wait("gp", dl_tt[tt][0], dl_tt[tt][1])
            if out_n >= 2:
                k.wait("gp", f"dO{out_n % 2}", 16 * (out_n // 2))
            k.op("gp", lambda e, tt=tt, gb=gb: e.tensor_tensor(
                otl.ap()[:, tt % 2, :], sin.ap()[:, tt % 2, :], gb.ap(), ALU.mult))
            k.op("gp", lambda e: e.drain())
            k.op("gp", lambda e, tt=tt: e.tensor_tensor(
                otl.ap()[:, tt % 2, :], otl.ap()[:, tt % 2, :], xin.ap()[:, tt % 2, :],
                ALU.add))
            k.drain_inc("gp", "gq")
            ve_comb[tt % 2] = k.cnt["gq"]
            ve_comb_last = k.cnt["gq"]
            k.wait("sy", "gq", ve_comb[tt % 2])
            k.dma("sy", lambda g=g, tt=tt: out_d.ap()[g * 128:(g + 1) * 128, :],
                  lambda tt=tt: otl.ap()[:, tt % 2, :], f"dO{out_n % 2}")
            out_n += 1
            if tt + 2 < 4:
                emit_sx_load(tt + 2)

    k.wait("sy", "dO0", k.cnt["dO0"])
    k.wait("sy", "dO1", k.cnt["dO1"])
    k.wait("sy", "dS", k.cnt["dS"])
    k.wait("sy", "dL0", k.cnt["dL0"])
    k.wait("sy", "dL1", k.cnt["dL1"])
    for s in ("dX0", "dX1", "dWq0", "dWq1", "dWv", "dWo", "dF0", "dF1", "dD0", "dD1"):
        k.wait("sy", s, k.cnt[s])
    k.wait("gp", "dG", k.cnt["dG"])

    import contextlib
    with contextlib.ExitStack() as stack:
        sem_objs = {s: stack.enter_context(nc.semaphore(s)) for s in k.cnt}
        block = stack.enter_context(nc.Block())

        def body(eng_name):
            def run(eng):
                for item in k.ops[eng_name]:
                    if item[0] == "w":
                        eng.wait_ge(sem_objs[item[1]], item[2])
                    elif item[0] == "d":
                        eng.drain().then_inc(sem_objs[item[1]], item[2])
                    else:
                        _, fn, inc, amt = item
                        ins = fn(eng)
                        if inc is not None:
                            ins.then_inc(sem_objs[inc], amt)
            return run
        block.sync(body("sy"))
        block.gpsimd(body("gp"))
        block.tensor(body("te"))
        block.vector(body("ve"))
        block.scalar(body("se"))
    return nc


_PROG_CACHE = {}


def kernel(L0_attn_x, L0_attn_w_qkv, L0_attn_w_out, L0_attn_b_out,
           L0_ffn_w_gate, L0_ffn_w_up, L0_ffn_w_down):
    x = np.ascontiguousarray(np.asarray(L0_attn_x, dtype=np.float32))
    wqkvT = np.ascontiguousarray(np.asarray(L0_attn_w_qkv, dtype=np.float32).T)
    woutT = np.ascontiguousarray(np.asarray(L0_attn_w_out, dtype=np.float32).T)
    brow = np.ascontiguousarray(np.asarray(L0_attn_b_out, dtype=np.float32).reshape(1, D))
    wgT = np.ascontiguousarray(np.asarray(L0_ffn_w_gate, dtype=np.int32).T)
    wuT = np.ascontiguousarray(np.asarray(L0_ffn_w_up, dtype=np.int32).T)
    wdT = np.ascontiguousarray(np.asarray(L0_ffn_w_down, dtype=np.int32).T)
    ident = np.eye(128, dtype=np.float32)
    mask = np.where(np.arange(128)[None, :] > np.arange(128)[:, None],
                    np.float32(NEG), np.float32(0.0)).astype(np.float32)

    if MODE not in _PROG_CACHE:
        _PROG_CACHE[MODE] = build_program(MODE)
    nc = _PROG_CACHE[MODE]

    in_maps = []
    for c in range(8):
        in_maps.append({
            "x": np.ascontiguousarray(x[c]),
            "wqkvT": wqkvT, "woutT": woutT, "brow": brow,
            "wgT": wgT, "wuT": wuT, "wdT": wdT,
            "ident": ident, "mask": mask,
        })
    res = run_bass_kernel_spmd(nc, in_maps, core_ids=list(range(8)))
    return np.stack([res.results[c]["out"] for c in range(8)], axis=0)


# version 35 (final)
# speedup vs baseline: 1.0472x; 1.0220x over previous
import math
import numpy as np

import concourse.bass as bass
import concourse.mybir as mybir
import concourse.bass_utils as _bu
from concourse.bass_utils import run_bass_kernel_spmd

if not getattr(_bu, "_ldw_patched", False):
    _orig_run_command = _bu.run_command

    def _run_command_ldw(argv, **kw):
        argv = ["--enable-ldw-opt=true" if a == "--enable-ldw-opt=false" else a
                for a in argv]
        return _orig_run_command(argv, **kw)

    _bu.run_command = _run_command_ldw
    _bu._ldw_patched = True

F32 = mybir.dt.float32
BF16 = mybir.dt.bfloat16
FP16 = mybir.dt.float16
I32 = mybir.dt.int32
AF = mybir.ActivationFunctionType
ALU = mybir.AluOpType
AX = mybir.AxisListType

MODE = "split"

T = 2048
D = 1024
DQ = 512
HID = 3072
NT = T // 128
KD = D // 128
KQ = DQ // 128
NH = HID // 128
SLAB = 512
NSLAB = T // SLAB
EPS = 1e-5
C_ATT = 1.0 / math.sqrt(DQ)
C127 = 1.0 / 127.0
C127SQ = 1.0 / (127.0 * 127.0)
NEG = -1.0e30
SA = 64.0
SP = 16.0
SG = 1.0 / 16.0
SD = 64.0
SH = 1.0 / 64.0

ENGINES = ("sy", "gp", "te", "ve", "se")


class Builder:

    def __init__(self):
        self.ops = {e: [] for e in ENGINES}
        self.cnt = {}

    def sem(self, name):
        self.cnt[name] = 0

    def wait(self, eng, sem, val):
        assert val <= self.cnt[sem], (eng, sem, val, self.cnt[sem])
        if val > 0:
            self.ops[eng].append(("w", sem, val))

    def op(self, eng, fn, inc=None, amt=1):
        self.ops[eng].append(("c", fn, inc, amt))
        if inc is not None:
            self.cnt[inc] += amt

    def drain_inc(self, eng, sem, amt=1):
        self.ops[eng].append(("d", sem, amt))
        self.cnt[sem] += amt

    def dma(self, eng, out_ap_fn, in_ap_fn, sem):
        def fn(e):
            return e.dma_start(out=out_ap_fn(), in_=in_ap_fn())
        self.op(eng, fn, inc=sem, amt=16)


def build_program(mode):
    split = mode == "split"
    nc = bass.Bass("TRN2", target_bir_lowering=False)

    x_d = nc.dram_tensor("x", [T, D], F32, kind="ExternalInput")
    wqkvT_d = nc.dram_tensor("wqkvT", [D, 3 * DQ], F32, kind="ExternalInput")
    woutT_d = nc.dram_tensor("woutT", [DQ, D], F32, kind="ExternalInput")
    brow_d = nc.dram_tensor("brow", [1, D], F32, kind="ExternalInput")
    wgT_d = nc.dram_tensor("wgT", [D, HID], I32, kind="ExternalInput")
    wuT_d = nc.dram_tensor("wuT", [D, HID], I32, kind="ExternalInput")
    wdT_d = nc.dram_tensor("wdT", [HID, D], I32, kind="ExternalInput")
    ident_d = nc.dram_tensor("ident", [128, 128], F32, kind="ExternalInput")
    mask_d = nc.dram_tensor("mask", [128, 128], F32, kind="ExternalInput")
    out_d = nc.dram_tensor("out", [T, D], F32, kind="ExternalOutput")
    sscr_d = nc.dram_tensor("sscr", [T, D], F32)
    WT = FP16 if split else F32
    wgb_d = nc.dram_tensor("wgb_scr", [D, HID], WT)
    wub_d = nc.dram_tensor("wub_scr", [D, HID], WT)
    wdb_d = nc.dram_tensor("wdb_scr", [HID, D], WT)

    SBUF_BASE = 16896
    off = [SBUF_BASE]

    def place(name, shape, dtype, at=None):
        nbytes = int(np.prod(shape[1:])) * mybir.dt.size(dtype)
        o = off[0] if at is None else at
        o = (o + 31) // 32 * 32
        t = nc.alloc_sbuf_tensor_at(name, list(shape), dtype, offset=o)
        end = o + nbytes
        off[0] = max(off[0], end)
        return t

    ident = place("identf", [128, 128], F32)
    maskt = place("maskt", [128, 128], F32)
    bbc_off = (off[0] + 31) // 32 * 32
    b_bc = place("b_bc", [128, D], F32)
    brow = place("brow_s", [1, D], F32, at=bbc_off)
    ones1 = place("ones1", [1, 128], F32)
    st12 = place("st12", [128, 2, 12], F32)
    mv = place("mv", [128, 2, 2], F32)
    stdv = place("stdv", [128, 2], F32)
    inv = place("inv", [128, 2], F32)
    negmi = place("negmi", [128, 2], F32)
    negmax = place("negmax", [128, 1], F32)
    negcm = place("negcm", [128, 1], F32)
    rsum = place("rsum", [128, 1], F32)
    invr = place("invr", [128, 1], F32)
    epsb = place("epsb", [128, 1], F32)
    zeroc = place("zeroc", [128, 1], F32)

    if split:
        xnT = (place("xnT_hi", [128, KD, T], FP16),
               place("xnT_lo", [128, KD, T], FP16))
    else:
        xnT = (place("xnT", [128, KD, T], F32),)

    r2_base = off[0]
    if split:
        qT = (place("qT_hi", [128, KQ, T], FP16), place("qT_lo", [128, KQ, T], FP16))
        kT = (place("kT_hi", [128, KQ, T], FP16), place("kT_lo", [128, KQ, T], FP16))
        vtk = (place("v_hi", [128, NT, DQ], FP16), place("v_lo", [128, NT, DQ], FP16))
    else:
        qT = (place("qT", [128, KQ, T], F32),)
        kT = (place("kT", [128, KQ, T], F32),)
        vtk = (place("vtk", [128, NT, DQ], F32),)
    r2_end_a = off[0]
    wvf = place("wvf", [128, KD, DQ], F32, at=r2_base + 32768)
    if split:
        wvh = place("wvh", [128, KD, DQ], FP16, at=r2_base + 49152)
        wvl = place("wvl", [128, KD, DQ], FP16, at=r2_base + 57344)
    off[0] = r2_base
    if split:
        hT = (place("hT_hi", [128, NH, SLAB], FP16, at=r2_base),
              place("hT_lo", [128, NH, SLAB], FP16))
        wgb = place("wgb", [128, 2, KD, 128], FP16)
        wub = place("wub", [128, 2, KD, 128], FP16)
        wdb = place("wdb", [128, 2, D], FP16)
    else:
        hT = (place("hT", [128, NH, SLAB], F32, at=r2_base),)
        wgb = place("wgb", [128, 2, KD, 128], F32)
        wub = place("wub", [128, 2, KD, 128], F32)
        wdb = place("wdb", [128, 2, D], F32)
    off[0] = max(off[0], r2_end_a)

    r3_base = off[0]
    r3 = r3_base

    def at(kb):
        return r3 + int(kb * 1024)

    xt = place("xt", [128, 2, D], F32, at=at(0))
    xnsc = place("xnsc", [128, 2, D], F32, at=at(8))
    pfull = place("pfull", [128, T], F32, at=at(0))
    if split:
        pT = (place("pT_hi", [128, NT, 128], FP16, at=at(8)),
              place("pT_lo", [128, NT, 128], FP16, at=at(12)))
    else:
        pT = (place("pT", [128, NT, 128], F32, at=at(8)),)

    wqf = place("wqf", [128, 2, KD, 128], F32, at=at(16))
    if split:
        wqh = place("wqh", [128, 2, KD, 128], FP16, at=at(24))
        wql = place("wql", [128, 2, KD, 128], FP16, at=at(28))
        wo = (place("wo_hi", [128, KQ, D], FP16, at=at(16)),
              place("wo_lo", [128, KQ, D], FP16, at=at(24)))
    else:
        wof = place("wof", [128, KQ, D], F32, at=at(16))
        wo = (wof,)

    if split:
        oaT = (place("oaT_hi", [128, KQ, 128], FP16, at=at(33)),
               place("oaT_lo", [128, KQ, 128], FP16, at=at(34)))
    else:
        oaT = (place("oaT", [128, KQ, 128], F32, at=at(33)),)
    oaf = place("oaf", [128, DQ], F32, at=at(35))
    stl = place("stl", [128, D], F32, at=at(37))
    if split:
        wof = place("wof_st", [128, 1, D], F32, at=at(37))
        wof2 = place("wof_st2", [128, 1, D], F32, at=at(33))
    sin = place("sin", [128, 2, D], F32, at=at(0))
    xin = place("xin", [128, 2, D], F32, at=at(8))
    otl = place("otl", [128, 2, D], F32, at=at(16))
    gtl = place("gtl", [128, D], F32, at=at(24))
    hsc = place("hsc", [128, SLAB], F32, at=at(28))
    if split:
        hfs = place("hfs", [128, SLAB], F32, at=at(30))
    gtl2 = place("gtl2", [128, D], F32, at=at(37))
    off[0] = max(off[0], at(41))
    assert off[0] <= 229200, f"SBUF over budget: {off[0]}"

    psA = nc.alloc_psum_tensor("psA", [128, 2048], F32)
    psB = nc.alloc_psum_tensor("psB", [128, 1024], F32)
    psC = nc.alloc_psum_tensor("psC", [128, 1024], F32)

    k = Builder()
    for s in ("dC", "dG", "dX0", "dX1", "dWq0", "dWq1", "dWv", "dWo", "vw", "dS", "dL0", "dL1", "dO0", "dO1",
              "dF0", "dF1", "dD0", "dD1", "pe", "ve", "se"):
        k.sem(s)


    k.dma("sy", lambda: ident.ap(), lambda: ident_d.ap(), "dC")
    k.dma("sy", lambda: maskt.ap(), lambda: mask_d.ap(), "dC")
    k.dma("sy", lambda: brow.ap(), lambda: brow_d.ap(), "dC")
    k.wait("ve", "dC", 48)
    k.op("ve", lambda e: e.memset(ones1.ap(), 1.0))
    k.op("ve", lambda e: e.memset(epsb.ap(), EPS))
    k.op("ve", lambda e: e.memset(zeroc.ap(), 0.0))
    k.drain_inc("ve", "ve")
    ve_ones = k.cnt["ve"]
    k.wait("te", "dC", 48)
    k.wait("te", "ve", ve_ones)
    for n in range(2):
        k.op("te", lambda e, n=n: e.matmul(
            psB.ap()[:, n * 512:(n + 1) * 512], ones1.ap(),
            brow.ap()[:, n * 512:(n + 1) * 512], start=True, stop=True),
            inc=("pe" if n == 1 else None))
    k.wait("ve", "pe", k.cnt["pe"])
    k.op("ve", lambda e: e.tensor_copy(b_bc.ap(), psB.ap()))
    k.drain_inc("ve", "ve")
    ve_bbc = k.cnt["ve"]

    def mm_group(psum_ap_fn, terms, inc="pe"):
        n = len(terms)
        for j, (lf, rf) in enumerate(terms):
            k.op("te", lambda e, lf=lf, rf=rf, j=j, n=n, pf=psum_ap_fn: e.matmul(
                pf(), lf(), rf(), start=(j == 0), stop=(j == n - 1)),
                inc=(inc if j == n - 1 else None))

    def passes(a_bufs, b_bufs):
        if len(a_bufs) == 1 and len(b_bufs) == 1:
            return [(a_bufs[0], b_bufs[0])]
        if len(a_bufs) == 1:
            return [(a_bufs[0], b_bufs[0]), (a_bufs[0], b_bufs[1])]
        if len(b_bufs) == 1:
            return [(a_bufs[0], b_bufs[0]), (a_bufs[1], b_bufs[0])]
        return [(a_bufs[0], b_bufs[0]), (a_bufs[0], b_bufs[1]), (a_bufs[1], b_bufs[0])]

    x_r = x_d.ap().rearrange("(i p) d -> i p d", p=128)
    wq_r = wqkvT_d.ap().rearrange("(c p) e -> p c e", p=128)
    se_lnapply = [None] * NT
    ve_xn_evac = [None] * NT
    pe_xn_tr = [None] * NT
    pe_xn_tr_half = [None] * NT
    ve_wsplit = [None] * 8
    te_wuse = [None] * 8
    ve_evac_qk = {}
    qk_rr = [0]

    for i in range(2):
        k.dma("sy", lambda i=i: xt.ap()[:, i % 2, :], lambda i=i: x_r[i], f"dX{i % 2}")
    k.dma("sy", lambda: wvf.ap(), lambda: wq_r[:, :, 1024:1536], "dWv")

    def emit_wq_load(mi):
        if mi >= 2:
            if split:
                k.wait("sy", "ve", ve_wsplit[mi - 2])
            else:
                k.wait("sy", "pe", te_wuse[mi - 2])
        k.dma("sy", lambda mi=mi: wqf.ap()[:, mi % 2],
              lambda mi=mi: wq_r[:, :, mi * 128:(mi + 1) * 128], f"dWq{mi % 2}")

    def emit_wsplit(mi):
        if split:
            k.wait("ve", f"dWq{mi % 2}", 16 * (mi // 2 + 1))
            if mi >= 2:
                k.wait("ve", "pe", te_wuse[mi - 2])
            k.op("ve", lambda e, mi=mi: e.tensor_scalar_mul(wqh.ap()[:, mi % 2],
                                                            wqf.ap()[:, mi % 2], SA))
            k.op("ve", lambda e: e.drain())
            k.op("ve", lambda e, mi=mi: e.scalar_tensor_tensor(
                wql.ap()[:, mi % 2], wqf.ap()[:, mi % 2], SA, wqh.ap()[:, mi % 2],
                ALU.mult, ALU.subtract))
            k.drain_inc("ve", "ve")
        ve_wsplit[mi] = k.cnt["ve"]

    def emit_wv_split(kc_lo, kc_hi):
        if not split:
            return
        k.wait("ve", "dWv", 16)
        k.op("ve", lambda e, kc_lo=kc_lo, kc_hi=kc_hi: e.tensor_scalar_mul(
            wvh.ap()[:, kc_lo:kc_hi], wvf.ap()[:, kc_lo:kc_hi], SA))
        k.op("ve", lambda e: e.drain())
        k.op("ve", lambda e, kc_lo=kc_lo, kc_hi=kc_hi: e.scalar_tensor_tensor(
            wvl.ap()[:, kc_lo:kc_hi], wvf.ap()[:, kc_lo:kc_hi], SA,
            wvh.ap()[:, kc_lo:kc_hi], ALU.mult, ALU.subtract))
        k.drain_inc("ve", "ve")

    ve_wv_split = [0]
    ve_evac_v = ve_evac_qk

    def emit_v_group(hh, tc, xn_ready):
        k.wait("te", xn_ready[0], xn_ready[1])
        if split:
            k.wait("te", "ve", ve_wv_split[0])
            wvb = [lambda kc: wvh.ap()[:, kc, hh * 256:(hh + 1) * 256],
                   lambda kc: wvl.ap()[:, kc, hh * 256:(hh + 1) * 256]]
        else:
            k.wait("te", "dWv", 16)
            wvb = [lambda kc: wvf.ap()[:, kc, hh * 256:(hh + 1) * 256]]
        bank = qk_rr[0] % 4
        qk_rr[0] += 1
        prev = ve_evac_qk.get(bank)
        if prev is not None:
            k.wait("te", prev[0], prev[1])
        terms = []
        for kc in range(KD):
            for (xb, wv) in passes([
                    (lambda t: (lambda kc, tc=tc: t.ap()[:, kc, tc * 128:(tc + 1) * 128]))(t)
                    for t in xnT], wvb):
                terms.append(((lambda xb=xb, kc=kc: xb(kc)),
                              (lambda wv=wv, kc=kc: wv(kc))))
        mm_group(lambda bank=bank: psA.ap()[:, bank * 512: bank * 512 + 256], terms)
        pe_g = k.cnt["pe"]
        k.wait("se", "pe", pe_g)
        k.op("se", lambda e, tc=tc, hh=hh, bank=bank: e.activation(
            vtk[0].ap()[:, tc, hh * 256:(hh + 1) * 256],
            psA.ap()[:, bank * 512: bank * 512 + 256], AF.Identity, bias=zeroc.ap()))
        k.drain_inc("se", "se")
        se_c = k.cnt["se"]
        if split:
            k.wait("ve", "se", se_c)
            k.op("ve", lambda e, tc=tc, hh=hh, bank=bank: e.tensor_tensor(
                vtk[1].ap()[:, tc, hh * 256:(hh + 1) * 256],
                psA.ap()[:, bank * 512: bank * 512 + 256],
                vtk[0].ap()[:, tc, hh * 256:(hh + 1) * 256], ALU.subtract))
            k.drain_inc("ve", "ve")
            ve_evac_qk[bank] = ("ve", k.cnt["ve"])
        else:
            ve_evac_qk[bank] = ("se", se_c)

    def emit_qk_group(mi, n, xn_ready):
        k.wait("te", xn_ready[0], xn_ready[1])
        if split:
            k.wait("te", "ve", ve_wsplit[mi])
            wbufs = [lambda kc, mi=mi: wqh.ap()[:, mi % 2, kc, :],
                     lambda kc, mi=mi: wql.ap()[:, mi % 2, kc, :]]
        else:
            k.wait("te", f"dWq{mi % 2}", 16 * (mi // 2 + 1))
            wbufs = [lambda kc, mi=mi: wqf.ap()[:, mi % 2, kc, :]]
        bank = qk_rr[0] % 4
        qk_rr[0] += 1
        prev = ve_evac_qk.get(bank)
        if prev is not None:
            k.wait("te", prev[0], prev[1])
        terms = []
        for kc in range(KD):
            for (wa, xb) in passes(wbufs, [
                    (lambda t: (lambda kc, n: t.ap()[:, kc, n * 512:(n + 1) * 512]))(t)
                    for t in xnT]):
                terms.append(((lambda wa=wa, kc=kc: wa(kc)),
                              (lambda xb=xb, kc=kc, n=n: xb(kc, n))))
        mm_group(lambda bank=bank: psA.ap()[:, bank * 512:(bank + 1) * 512], terms)
        pe_g = k.cnt["pe"]
        dst = qT if mi < 4 else kT
        m = mi % 4
        k.wait("se", "pe", pe_g)
        k.op("se", lambda e, dst=dst, m=m, n=n, bank=bank: e.activation(
            dst[0].ap()[:, m, n * 512:(n + 1) * 512],
            psA.ap()[:, bank * 512:(bank + 1) * 512], AF.Identity, bias=zeroc.ap()))
        k.drain_inc("se", "se")
        se_c = k.cnt["se"]
        if split:
            k.wait("ve", "se", se_c)
            k.op("ve", lambda e, dst=dst, m=m, n=n, bank=bank: e.tensor_tensor(
                dst[1].ap()[:, m, n * 512:(n + 1) * 512],
                psA.ap()[:, bank * 512:(bank + 1) * 512],
                dst[0].ap()[:, m, n * 512:(n + 1) * 512], ALU.subtract))
            k.drain_inc("ve", "ve")
            ve_evac_qk[bank] = ("ve", k.cnt["ve"])
        else:
            ve_evac_qk[bank] = ("se", se_c)

    emit_wq_load(0)
    emit_wq_load(1)

    def emit_ln_chain(j):
        p = j % 2
        k.wait("ve", f"dX{p}", 16 * (j // 2 + 1))
        if j >= 2:
            k.wait("ve", "se", se_lnapply[j - 2])
        k.op("ve", lambda e, p=p: e.bn_stats(st12.ap()[:, p, 0:6], xt.ap()[:, p, 0:512]))
        k.op("ve", lambda e, p=p: e.bn_stats(st12.ap()[:, p, 6:12], xt.ap()[:, p, 512:1024]))
        k.op("ve", lambda e: e.drain())
        k.op("ve", lambda e, p=p: e.bn_aggr(mv.ap()[:, p, :], st12.ap()[:, p, :]))
        k.drain_inc("ve", "ve")
        ve_mv = k.cnt["ve"]
        k.wait("se", "ve", ve_mv)
        k.op("se", lambda e, p=p: e.activation(stdv.ap()[:, p:p + 1], mv.ap()[:, p, 1:2],
                                               AF.Sqrt, bias=epsb.ap()))
        k.drain_inc("se", "se")
        se_std = k.cnt["se"]
        k.wait("ve", "se", se_std)
        k.op("ve", lambda e, p=p: e.reciprocal(inv.ap()[:, p:p + 1], stdv.ap()[:, p:p + 1]))
        k.op("ve", lambda e: e.drain())
        k.op("ve", lambda e, p=p: e.tensor_scalar(negmi.ap()[:, p:p + 1], mv.ap()[:, p, 0:1],
                                                  inv.ap()[:, p:p + 1], -1.0,
                                                  ALU.mult, ALU.mult))
        k.drain_inc("ve", "ve")
        ve_ln = k.cnt["ve"]
        k.wait("se", "ve", ve_ln)
        if j >= 2:
            k.wait("se", "pe", pe_xn_tr[j - 2])
        k.op("se", lambda e, p=p: e.activation(xnsc.ap()[:, p, :], xt.ap()[:, p, :],
                                               AF.Identity, bias=negmi.ap()[:, p:p + 1],
                                               scale=inv.ap()[:, p:p + 1]))
        k.drain_inc("se", "se")
        se_lnapply[j] = k.cnt["se"]
        if j + 2 < NT:
            k.wait("sy", "se", se_lnapply[j])
            k.dma("sy", lambda j=j: xt.ap()[:, j % 2, :], lambda j=j: x_r[j + 2], f"dX{j % 2}")

    def emit_xn_evac(j):
        pe_need = pe_xn_tr_half[j]
        ps = psC if j % 2 == 0 else psB
        for half in range(2):
            k.wait("se", "pe", pe_need[half])
            src = ps.ap().rearrange("p (h c n) -> p h c n", h=2, c=4)[:, half]
            lo_s = half * 4
            k.op("se", lambda e, j=j, lo_s=lo_s, src=src: e.activation(
                xnT[0].ap()[:, lo_s:lo_s + 4, j * 128:(j + 1) * 128], src,
                AF.Identity, bias=zeroc.ap()))
            k.drain_inc("se", "se")
            se_c = k.cnt["se"]
            if split:
                k.wait("ve", "se", se_c)
                k.op("ve", lambda e, j=j, lo_s=lo_s, src=src: e.tensor_tensor(
                    xnT[1].ap()[:, lo_s:lo_s + 4, j * 128:(j + 1) * 128], src,
                    xnT[0].ap()[:, lo_s:lo_s + 4, j * 128:(j + 1) * 128], ALU.subtract))
                k.drain_inc("ve", "ve")
        ve_xn_evac[j] = ("ve", k.cnt["ve"]) if split else ("se", k.cnt["se"])

    emit_ln_chain(0)
    emit_ln_chain(1)
    emit_wv_split(0, 4)
    emit_wv_split(4, 8)
    ve_wv_split[0] = k.cnt["ve"]
    for i in range(NT):
        k.wait("te", "se", se_lnapply[i])
        ps = psC if i % 2 == 0 else psB
        if i == 1:
            k.wait("te", "ve", ve_bbc)
        pe_need = [None, None]
        for half in range(2):
            if i >= 2:
                k.wait("te", ve_xn_evac[i - 2][0], ve_xn_evac[i - 2][1])
            for c4 in range(4):
                c = half * 4 + c4
                k.op("te", lambda e, half=half, c4=c4, c=c, p=i % 2, ps=ps: e.transpose(
                    ps.ap()[:, half * 512 + c4 * 128: half * 512 + (c4 + 1) * 128],
                    xnsc.ap()[:, p, c * 128:(c + 1) * 128], ident.ap()),
                    inc=("pe" if c4 == 3 else None))
            pe_need[half] = k.cnt["pe"]
        pe_xn_tr_half[i] = pe_need
        pe_xn_tr[i] = k.cnt["pe"]
        if i >= 1 and (i - 1) % 4 == 3:
            n = (i - 1) // 4
            emit_qk_group(0, n, ve_xn_evac[i - 1])
            emit_qk_group(1, n, ve_xn_evac[i - 1])
        if i == 2:
            emit_v_group(0, 0, ve_xn_evac[0])
            emit_v_group(1, 0, ve_xn_evac[0])
        if 2 <= i <= 15:
            tc = i - 1
            emit_v_group(0, tc, ve_xn_evac[tc])
            emit_v_group(1, tc, ve_xn_evac[tc])
        emit_xn_evac(i)
        if i + 2 < NT:
            emit_ln_chain(i + 2)
        if i == 1:
            emit_wsplit(0)
            emit_wsplit(1)
    emit_qk_group(0, 3, ve_xn_evac[NT - 1])
    emit_qk_group(1, 3, ve_xn_evac[NT - 1])
    emit_v_group(0, NT - 1, ve_xn_evac[NT - 1])
    emit_v_group(1, NT - 1, ve_xn_evac[NT - 1])
    te_wuse[0] = te_wuse[1] = k.cnt["pe"]
    ve_xn_done = k.cnt["ve"]


    emit_wq_load(2)
    emit_wq_load(3)
    xn_all = ve_xn_evac[NT - 1]
    emit_wsplit(2)
    for mi in range(2, 8):
        if mi + 1 < 8:
            emit_wsplit(mi + 1)
        for n in range(4):
            emit_qk_group(mi, n, xn_all)
        te_wuse[mi] = k.cnt["pe"]
        if mi + 2 < 8:
            emit_wq_load(mi + 2)
    ve_p1_done = k.cnt["ve"]
    se_p1_done = k.cnt["se"]

    se_exp = [None] * NT
    pe_ptr_done = [None] * NT
    pe_scores_l = [None] * NT
    ve_sevac_cnt = None
    last_psC_evac = {0: ("ve", k.cnt["ve"]), 1: ("ve", k.cnt["ve"])}
    psb5_free = [None]
    ve_sevac_b4 = None

    def emit_scores(i):
        ncols = (i + 1) * 128
        nblk = (ncols + 511) // 512
        if prv.get(i) is None:
            k.wait("te", "ve", ve_p1_done)
            k.wait("te", "se", se_p1_done)
        else:
            k.wait("te", "se", se_exp[prv[i]])
        for b in range(nblk):
            nb = min(512, ncols - b * 512)
            terms = []
            for kc in range(KQ):
                for (qa, ka) in passes(
                        [(lambda t: (lambda kc, i=i: t.ap()[:, kc, i * 128:(i + 1) * 128]))(t)
                         for t in qT],
                        [(lambda t: (lambda kc, b=b, nb=nb: t.ap()[:, kc, b * 512: b * 512 + nb]))(t)
                         for t in kT]):
                    terms.append(((lambda qa=qa, kc=kc: qa(kc)),
                                  (lambda ka=ka, kc=kc: ka(kc))))
            mm_group(lambda b=b, nb=nb: psA.ap()[:, b * 512: b * 512 + nb], terms)
        pe_scores_l[i] = k.cnt["pe"]

    def softprep_exp(j):
        ncols = (j + 1) * 128
        k.wait("ve", "pe", pe_scores_l[j])
        k.op("ve", lambda e, j=j: e.tensor_tensor(
            psA.ap()[:, j * 128:(j + 1) * 128], psA.ap()[:, j * 128:(j + 1) * 128],
            maskt.ap(), ALU.add))
        k.op("ve", lambda e: e.drain())
        k.op("ve", lambda e, ncols=ncols: e.reduce_max(
            negmax.ap(), psA.ap()[:, 0:ncols], axis=AX.X, negate=True))
        k.op("ve", lambda e: e.drain())
        k.op("ve", lambda e: e.tensor_scalar_mul(negcm.ap(), negmax.ap(),
                                                 C_ATT / (SA * SA) if split else C_ATT))
        k.drain_inc("ve", "ve")
        ve_sm = k.cnt["ve"]
        k.wait("se", "ve", ve_sm)
        if prv.get(j) is not None:
            k.wait("se", "pe", pe_ptr_done[prv[j]])
        k.op("se", lambda e, ncols=ncols: e.activation(
            pfull.ap()[:, 0:ncols], psA.ap()[:, 0:ncols], AF.Exp,
            bias=negcm.ap(), scale=(C_ATT / (SA * SA) if split else C_ATT),
            accum_out=rsum.ap()))
        k.drain_inc("se", "se")
        se_exp[j] = k.cnt["se"]

    tile_order = []
    for z in range(NT // 2):
        tile_order += [z, NT - 1 - z]
    store_pos = {g: idx for idx, g in enumerate(tile_order)}
    nxt = {tile_order[z]: tile_order[z + 1] for z in range(NT - 1)}
    prv = {tile_order[z + 1]: tile_order[z] for z in range(NT - 1)}

    emit_scores(tile_order[0])
    softprep_exp(tile_order[0])

    wo_r = woutT_d.ap().rearrange("(c p) o -> p c o", p=128)
    if split:
        stgs = [wof, wof2]
        for hw in range(KQ):
            if hw >= 2:
                k.wait("se", "vw", hw - 1)
            k.dma("se", lambda hw=hw: stgs[hw % 2].ap(),
                  lambda hw=hw: wo_r[:, hw: hw + 1, :], "dWo")
            k.wait("se", "dWo", k.cnt["dWo"])
            k.wait("se", "ve", ve_wsplit[7])
            k.op("se", lambda e, hw=hw: e.activation(
                wo[0].ap()[:, hw: hw + 1, :], stgs[hw % 2].ap(), AF.Identity,
                bias=zeroc.ap(), scale=SD))
            k.drain_inc("se", "se")
            se_c = k.cnt["se"]
            k.wait("ve", "se", se_c)
            k.wait("ve", "pe", te_wuse[7])
            k.op("ve", lambda e, hw=hw: e.scalar_tensor_tensor(
                wo[1].ap()[:, hw: hw + 1, :], stgs[hw % 2].ap(), SD,
                wo[0].ap()[:, hw: hw + 1, :], ALU.mult, ALU.subtract))
            k.drain_inc("ve", "vw")
        dW_wo = k.cnt["dWo"]
    else:
        k.wait("se", "pe", te_wuse[7])
        k.dma("se", lambda: wof.ap(), lambda: wo_r, "dWo")
        dW_wo = k.cnt["dWo"]
    vw_done = k.cnt["vw"]

    for i in tile_order:
        ncols = (i + 1) * 128
        k.wait("ve", "se", se_exp[i])
        k.op("ve", lambda e: e.reciprocal(invr.ap(), rsum.ap()))
        if split:
            k.op("ve", lambda e: e.drain())
            k.op("ve", lambda e: e.tensor_scalar_mul(
                invr.ap(), invr.ap(), 1.0 / (SA * SP * SG * SD)))
        k.drain_inc("ve", "ve")
        if nxt.get(i) is not None:
            emit_scores(nxt[i])
        k.wait("te", "se", se_exp[i])
        pT_ready = []
        tr_slots = [psC.ap()[:, 0:512], psC.ap()[:, 512:1024], psB.ap()[:, 512:1024]]
        tr_free = [last_psC_evac[0], last_psC_evac[1], psb5_free[0]]
        for g in range((i + 4) // 4):
            sl = g % 3
            if tr_free[sl] is not None:
                k.wait("te", tr_free[sl][0], tr_free[sl][1])
            if sl == 2 and ve_sevac_cnt is not None:
                k.wait("te", "ve", ve_sevac_cnt)
            jmax = min(4, i + 1 - g * 4)
            for j4 in range(jmax):
                j = g * 4 + j4
                k.op("te", lambda e, sl=sl, j4=j4, j=j: e.transpose(
                    tr_slots[sl][:, j4 * 128:(j4 + 1) * 128],
                    pfull.ap()[:, j * 128:(j + 1) * 128], ident.ap()),
                    inc=("pe" if j4 == jmax - 1 else None))
            pe_tr = k.cnt["pe"]
            src = tr_slots[sl].rearrange("p (c n) -> p c n", c=4)[:, 0:jmax]
            g4 = g * 4
            k.wait("se", "pe", pe_tr)
            k.op("se", lambda e, g4=g4, jmax=jmax, src=src: e.activation(
                pT[0].ap()[:, g4: g4 + jmax, :], src, AF.Identity, bias=zeroc.ap(),
                scale=(SP if split else 1.0)))
            k.drain_inc("se", "se")
            se_c = k.cnt["se"]
            if split:
                k.wait("ve", "se", se_c)
                k.op("ve", lambda e, g4=g4, jmax=jmax, src=src: e.scalar_tensor_tensor(
                    pT[1].ap()[:, g4: g4 + jmax, :], src, SP,
                    pT[0].ap()[:, g4: g4 + jmax, :], ALU.mult, ALU.subtract))
                k.drain_inc("ve", "ve")
                done = ("ve", k.cnt["ve"])
            else:
                done = ("se", se_c)
            tr_free[sl] = done
            if sl < 2:
                last_psC_evac[sl] = done
            else:
                psb5_free[0] = done
            pT_ready.append(done)
        pe_ptr_done[i] = k.cnt["pe"]
        if ve_sevac_cnt is not None:
            k.wait("te", "ve", ve_sevac_b4)
        else:
            k.wait("te", ve_xn_evac[NT - 1][0], ve_xn_evac[NT - 1][1])
        nmm = (i + 1) * (3 if split else 1)
        emitted = 0
        for j in range(i + 1):
            if j % 4 == 0:
                rdy = pT_ready[j // 4]
                k.wait("te", rdy[0], rdy[1])
            for (pa, va) in passes(
                    [(lambda t: (lambda j: t.ap()[:, j, :]))(t) for t in pT],
                    [(lambda t: (lambda j: t.ap()[:, j, :]))(t) for t in vtk]):
                k.op("te", lambda e, pa=pa, va=va, j=j,
                     first=(emitted == 0), last=(emitted == nmm - 1): e.matmul(
                         psB.ap()[:, 0:512], pa(j), va(j), start=first, stop=last),
                     inc=("pe" if emitted == nmm - 1 else None))
                emitted += 1
        pe_av = k.cnt["pe"]
        k.wait("se", "pe", pe_av)
        k.op("se", lambda e: e.activation(oaf.ap(), psB.ap()[:, 0:512], AF.Identity,
                                          bias=zeroc.ap()))
        k.drain_inc("se", "se")
        se_oaf = k.cnt["se"]
        k.wait("te", "se", se_oaf)
        oat_slots = [psC.ap()[:, 0:512], psC.ap()[:, 512:1024], psB.ap()[:, 512:1024]]
        slot_free = [last_psC_evac[0], last_psC_evac[1], psb5_free[0]]
        slot_evac = [None] * 3
        oa_evac_m = [None] * KQ
        for m in range(KQ):
            sl = m % 3
            if slot_free[sl] is not None:
                k.wait("te", slot_free[sl][0], slot_free[sl][1])
            k.op("te", lambda e, m=m, sl=sl: e.transpose(
                oat_slots[sl][:, 0:128],
                oaf.ap()[:, m * 128:(m + 1) * 128], ident.ap()), inc="pe")
            pe_oat = k.cnt["pe"]
            k.wait("se", "pe", pe_oat)
            k.op("se", lambda e, m=m, sl=sl: e.activation(
                oaT[0].ap()[:, m, :], oat_slots[sl][:, 0:128],
                AF.Identity, bias=zeroc.ap(), scale=(SG if split else 1.0)))
            k.drain_inc("se", "se")
            se_c = k.cnt["se"]
            if split:
                k.wait("ve", "se", se_c)
                k.op("ve", lambda e, m=m, sl=sl: e.scalar_tensor_tensor(
                    oaT[1].ap()[:, m, :], oat_slots[sl][:, 0:128], SG,
                    oaT[0].ap()[:, m, :], ALU.mult, ALU.subtract))
                k.drain_inc("ve", "ve")
                slot_evac[sl] = ("ve", k.cnt["ve"])
            else:
                slot_evac[sl] = ("se", se_c)
            slot_free[sl] = slot_evac[sl]
            oa_evac_m[m] = slot_evac[sl]
        last_psC_evac[0] = slot_evac[0]
        last_psC_evac[1] = slot_evac[1]
        if slot_evac[2] is not None:
            psb5_free[0] = slot_evac[2]
        if nxt.get(i) is not None:
            softprep_exp(nxt[i])
        if ve_sevac_cnt is None:
            if split:
                k.wait("te", "vw", vw_done)
            else:
                k.wait("te", "dWo", dW_wo)
        npass_o = 3 if split else 1
        for n2 in range(2):
            emitted = 0
            nmm = KQ * npass_o
            for kc in range(KQ):
                if n2 == 0:
                    rdy = oa_evac_m[kc]
                    k.wait("te", rdy[0], rdy[1])
                elif kc == 0:
                    k.wait("te", oa_evac_m[2][0], oa_evac_m[2][1])
                for (oa, wa) in passes(
                        [(lambda t: (lambda kc: t.ap()[:, kc, :]))(t) for t in oaT],
                        [(lambda t: (lambda kc, n2=n2: t.ap()[:, kc, n2 * 512:(n2 + 1) * 512]))(t)
                         for t in wo]):
                    k.op("te", lambda e, oa=oa, wa=wa, kc=kc, n2=n2,
                         first=(emitted == 0), last=(emitted == nmm - 1): e.matmul(
                             psB.ap()[:, n2 * 512:(n2 + 1) * 512], oa(kc), wa(kc),
                             start=first, stop=last),
                         inc=("pe" if emitted == nmm - 1 else None))
                    emitted += 1
        pe_s = k.cnt["pe"]
        k.wait("ve", "pe", pe_s)
        if store_pos[i] >= 1:
            k.wait("ve", "dS", 16 * store_pos[i])
        k.op("ve", lambda e: e.scalar_tensor_tensor(
            stl.ap()[:, 0:512], psB.ap()[:, 0:512], invr.ap(), b_bc.ap()[:, 0:512],
            ALU.mult, ALU.add))
        k.drain_inc("ve", "ve")
        ve_sevac_b4 = k.cnt["ve"]
        k.op("ve", lambda e: e.scalar_tensor_tensor(
            stl.ap()[:, 512:1024], psB.ap()[:, 512:1024], invr.ap(),
            b_bc.ap()[:, 512:1024], ALU.mult, ALU.add))
        k.drain_inc("ve", "ve")
        ve_sevac_cnt = k.cnt["ve"]
        k.wait("sy", "ve", ve_sevac_cnt)
        k.dma("sy", lambda i=i: sscr_d.ap()[i * 128:(i + 1) * 128, :], lambda: stl.ap(), "dS")
        sp = store_pos[i]
        if sp < 12:
            k.wait("gp", "dS", 16 * (sp + 1))
            if sp < 4:
                k.dma("gp", lambda sp=sp: wgb_d.ap()[sp * 256:(sp + 1) * 256, :],
                      lambda sp=sp: wgT_d.ap()[sp * 256:(sp + 1) * 256, :], "dG")
            elif sp < 8:
                j = sp - 4
                k.dma("gp", lambda j=j: wub_d.ap()[j * 256:(j + 1) * 256, :],
                      lambda j=j: wuT_d.ap()[j * 256:(j + 1) * 256, :], "dG")
            else:
                j = sp - 8
                k.dma("gp", lambda j=j: wdb_d.ap()[j * 768:(j + 1) * 768, :],
                      lambda j=j: wdT_d.ap()[j * 768:(j + 1) * 768, :], "dG")
    ve_p2_done = ve_sevac_cnt

    wg_r = wgb_d.ap().rearrange("(c p) h -> p c h", p=128)
    wu_r = wub_d.ap().rearrange("(c p) h -> p c h", p=128)
    wd_r = wdb_d.ap().rearrange("(c p) o -> p c o", p=128)

    k.wait("sy", "ve", ve_p2_done)
    k.wait("sy", "dG", 16 * 12)

    pe_gu_use = {}
    se_gate = {}
    ve_hml = None
    ve_up = {}
    pe_wd_use = {}
    pe_down_tt = [None] * 4
    se_gtl = {}
    ve_comb = {}
    ve_comb_last = None
    out_n = 0
    npass = 2 if split else 1
    dF_mark = {}
    dD_mark = {}
    dD_hist = {}

    def dD_mark_at(h):
        return dD_hist[h]

    def emit_gu_load(h):
        if pe_gu_use.get(h % 2) is not None:
            k.wait("sy", "pe", pe_gu_use[h % 2])
        k.dma("sy", lambda h=h: wgb.ap()[:, h % 2],
              lambda h=h: wg_r[:, :, h * 128:(h + 1) * 128], f"dF{h % 2}")
        k.dma("sy", lambda h=h: wub.ap()[:, h % 2],
              lambda h=h: wu_r[:, :, h * 128:(h + 1) * 128], f"dF{h % 2}")
        dF_mark[h] = (f"dF{h % 2}", k.cnt[f"dF{h % 2}"])

    pending_combs = []

    for s4 in range(NSLAB):
        if s4 == 0:
            emit_gu_load(0)
            emit_gu_load(1)
        for h in range(NH):
            if h == 2 and pending_combs:
                for f in pending_combs:
                    f()
                pending_combs.clear()
            dF_h = dF_mark[h]
            gbank = (h % 2) * 2
            ubank = (h % 2) * 2 + 1
            k.wait("te", dF_h[0], dF_h[1])
            if s4 == 0 and h == 0:
                k.wait("te", "se", se_exp[NT - 1])
            elif h <= 1 and se_gtl:
                for key in (0, 1):
                    if se_gtl.get(key) is not None:
                        k.wait("te", "se", se_gtl[key])
            if se_gate.get(gbank) is not None:
                k.wait("te", "se", se_gate[gbank])
            if ve_up.get(ubank) is not None:
                k.wait("te", "ve", ve_up[ubank])
            for wbuf, bank in ((wgb, gbank), (wub, ubank)):
                terms = []
                for kc in range(KD):
                    for (wa, xa) in passes(
                            [lambda kc, h=h, wbuf=wbuf: wbuf.ap()[:, h % 2, kc, :]],
                            [(lambda t: (lambda kc, s4=s4: t.ap()[:, kc, s4 * SLAB:(s4 + 1) * SLAB]))(t)
                             for t in xnT]):
                        terms.append(((lambda wa=wa, kc=kc: wa(kc)),
                                      (lambda xa=xa, kc=kc: xa(kc))))
                mm_group(lambda bank=bank: psA.ap()[:, bank * 512:(bank + 1) * 512], terms)
                if wbuf is wgb:
                    pe_gate = k.cnt["pe"]
            pe_up = k.cnt["pe"]
            pe_gu_use[h % 2] = pe_up
            k.wait("se", "pe", pe_gate)
            if ve_hml is not None:
                k.wait("se", "ve", ve_hml)
            k.op("se", lambda e, gbank=gbank: e.activation(
                hsc.ap(), psA.ap()[:, gbank * 512:(gbank + 1) * 512], AF.Silu,
                bias=zeroc.ap(), scale=C127))
            k.drain_inc("se", "se")
            se_gate[gbank] = k.cnt["se"]
            k.wait("ve", "se", se_gate[gbank])
            k.wait("ve", "pe", pe_up)
            if split:
                k.op("ve", lambda e, ubank=ubank: e.scalar_tensor_tensor(
                    hfs.ap(), psA.ap()[:, ubank * 512:(ubank + 1) * 512], SH,
                    hsc.ap(), ALU.mult, ALU.mult))
                k.op("ve", lambda e: e.drain())
                k.op("ve", lambda e, h=h: e.tensor_copy(hT[0].ap()[:, h, :], hfs.ap()))
                k.op("ve", lambda e: e.drain())
                k.op("ve", lambda e, h=h: e.tensor_tensor(
                    hT[1].ap()[:, h, :], hfs.ap(), hT[0].ap()[:, h, :], ALU.subtract))
            else:
                k.op("ve", lambda e, h=h, ubank=ubank: e.tensor_tensor(
                    hT[0].ap()[:, h, :], hsc.ap(), psA.ap()[:, ubank * 512:(ubank + 1) * 512],
                    ALU.mult))
            k.drain_inc("ve", "ve")
            ve_up[ubank] = k.cnt["ve"]
            ve_hml = k.cnt["ve"]
            if h + 2 < NH:
                emit_gu_load(h + 2)
            if h == NH - 2:
                for hh0 in (0, 1):
                    if pe_wd_use.get(hh0) is not None:
                        k.wait("sy", "pe", pe_wd_use[hh0])
                    k.dma("sy", lambda hh0=hh0: wdb.ap()[:, hh0],
                          lambda hh0=hh0: wd_r[:, hh0, :], f"dD{hh0}")
                    dD_hist[hh0] = (f"dD{hh0}", k.cnt[f"dD{hh0}"])
        ve_h_done = k.cnt["ve"]
        def psX_ap(tt, n2):
            if tt == 0:
                return psA.ap()[:, n2 * 512:(n2 + 1) * 512]
            if tt == 1:
                return psA.ap()[:, 1024 + n2 * 512: 1024 + (n2 + 1) * 512]
            if tt == 2:
                return psB.ap()[:, n2 * 512:(n2 + 1) * 512]
            return psC.ap()[:, n2 * 512:(n2 + 1) * 512]

        dl_tt = [None] * 4

        def emit_sx_load(tt):
            g = s4 * 4 + tt
            k.wait("sy", "dS", 16 * (store_pos[g] + 1))
            if ve_comb.get(tt % 2) is not None:
                k.wait("sy", "ve", ve_comb[tt % 2])
            k.dma("sy", lambda tt=tt, g=g: sin.ap()[:, tt % 2, :],
                  lambda g=g: sscr_d.ap()[g * 128:(g + 1) * 128, :], f"dL{tt % 2}")
            k.dma("sy", lambda tt=tt, g=g: xin.ap()[:, tt % 2, :],
                  lambda g=g: x_r[g], f"dL{tt % 2}")
            dl_tt[tt] = (f"dL{tt % 2}", k.cnt[f"dL{tt % 2}"])

        emit_sx_load(0)
        emit_sx_load(1)
        for key in (0, 2):
            if se_gate.get(key) is not None:
                k.wait("te", "se", se_gate[key])
        for key in (1, 3):
            if ve_up.get(key) is not None:
                k.wait("te", "ve", ve_up[key])
        if s4 == 0:
            k.wait("te", "ve", ve_p2_done)
            k.wait("te", last_psC_evac[0][0], last_psC_evac[0][1])
            k.wait("te", last_psC_evac[1][0], last_psC_evac[1][1])
        else:
            for key in (2, 3):
                if se_gtl.get(key) is not None:
                    k.wait("te", "se", se_gtl[key])
        k.wait("te", "ve", ve_h_done)
        if s4 + 1 < NSLAB:
            emit_gu_load(0)
            emit_gu_load(1)
        for h in range(NH):
            if h >= 2:
                if pe_wd_use.get(h % 2) is not None:
                    k.wait("sy", "pe", pe_wd_use[h % 2])
                k.dma("sy", lambda h=h: wdb.ap()[:, h % 2],
                      lambda h=h: wd_r[:, h, :], f"dD{h % 2}")
                dD_hist[h] = (f"dD{h % 2}", k.cnt[f"dD{h % 2}"])
            k.wait("te", dD_mark_at(h)[0], dD_mark_at(h)[1])
            for tt in range(4):
                hparts = [(lambda t: (lambda h, lt: t.ap()[:, h, lt * 128:(lt + 1) * 128]))(t)
                          for t in hT]
                for pi in range(npass):
                    for n2 in range(2):
                        ha = hparts[pi]
                        tt_done = (h == NH - 1 and pi == npass - 1 and n2 == 1)
                        k.op("te", lambda e, ha=ha, h=h, tt=tt, n2=n2, pi=pi,
                             first=(h == 0 and pi == 0),
                             last=(h == NH - 1 and pi == npass - 1): e.matmul(
                                 psX_ap(tt, n2),
                                 ha(h, tt),
                                 wdb.ap()[:, h % 2, n2 * 512:(n2 + 1) * 512],
                                 start=first, stop=last),
                             inc=("pe" if (tt_done or (tt == 3 and pi == npass - 1
                                                       and n2 == 1)) else None))
                        if tt_done:
                            pe_down_tt[tt] = k.cnt["pe"]
            pe_wd_use[h % 2] = k.cnt["pe"]
        pe_down = k.cnt["pe"]

        def emit_comb(tt, g, gb, pdt, dlt, s4=s4):
            nonlocal out_n, ve_comb_last
            k.wait("se", "pe", pdt)
            if s4 == 0 and tt == 1 and out_n == 1:
                k.wait("se", "dS", 16 * NT)
            if ve_comb.get(tt % 2) is not None:
                k.wait("se", "ve", ve_comb[tt % 2])
            c_dn = C127SQ / SH if split else C127SQ
            k.op("se", lambda e, tt=tt, gb=gb, c_dn=c_dn: e.activation(
                gb.ap()[:, 0:512], psX_ap(tt, 0), AF.Silu, bias=zeroc.ap(), scale=c_dn))
            k.op("se", lambda e, tt=tt, gb=gb, c_dn=c_dn: e.activation(
                gb.ap()[:, 512:1024], psX_ap(tt, 1), AF.Silu, bias=zeroc.ap(), scale=c_dn))
            k.drain_inc("se", "se")
            se_gtl[tt] = k.cnt["se"]
            k.wait("ve", "se", se_gtl[tt])
            k.wait("ve", dlt[0], dlt[1])
            if out_n >= 2:
                k.wait("ve", f"dO{out_n % 2}", 16 * (out_n // 2))
            k.op("ve", lambda e, tt=tt, gb=gb: e.tensor_tensor(
                otl.ap()[:, tt % 2, :], sin.ap()[:, tt % 2, :], gb.ap(), ALU.mult))
            k.op("ve", lambda e: e.drain())
            k.op("ve", lambda e, tt=tt: e.tensor_tensor(
                otl.ap()[:, tt % 2, :], otl.ap()[:, tt % 2, :], xin.ap()[:, tt % 2, :],
                ALU.add))
            k.drain_inc("ve", "ve")
            ve_comb[tt % 2] = k.cnt["ve"]
            ve_comb_last = k.cnt["ve"]
            k.wait("sy", "ve", ve_comb[tt % 2])
            k.dma("sy", lambda g=g, tt=tt: out_d.ap()[g * 128:(g + 1) * 128, :],
                  lambda tt=tt: otl.ap()[:, tt % 2, :], f"dO{out_n % 2}")
            out_n += 1

        for tt in range(4):
            g = s4 * 4 + tt
            gb = gtl if tt % 2 == 0 else gtl2
            if tt < 2:
                emit_comb(tt, g, gb, pe_down_tt[tt], dl_tt[tt])
                emit_sx_load(tt + 2)
            else:
                pending_combs.append(
                    lambda tt=tt, g=g, gb=gb, pdt=pe_down_tt[tt],
                    dlt=tuple(dl_tt[tt]), em=emit_comb: em(tt, g, gb, pdt, dlt))
    for f in pending_combs:
        f()
    pending_combs.clear()

    k.wait("sy", "dO0", k.cnt["dO0"])
    k.wait("sy", "dO1", k.cnt["dO1"])
    k.wait("sy", "dS", k.cnt["dS"])
    k.wait("sy", "dL0", k.cnt["dL0"])
    k.wait("sy", "dL1", k.cnt["dL1"])
    for s in ("dX0", "dX1", "dWq0", "dWq1", "dWv", "dWo", "dF0", "dF1", "dD0", "dD1"):
        k.wait("sy", s, k.cnt[s])
    k.wait("gp", "dG", k.cnt["dG"])

    import contextlib
    with contextlib.ExitStack() as stack:
        sem_objs = {s: stack.enter_context(nc.semaphore(s)) for s in k.cnt}
        block = stack.enter_context(nc.Block())

        def body(eng_name):
            def run(eng):
                for item in k.ops[eng_name]:
                    if item[0] == "w":
                        eng.wait_ge(sem_objs[item[1]], item[2])
                    elif item[0] == "d":
                        eng.drain().then_inc(sem_objs[item[1]], item[2])
                    else:
                        _, fn, inc, amt = item
                        ins = fn(eng)
                        if inc is not None:
                            ins.then_inc(sem_objs[inc], amt)
            return run
        block.sync(body("sy"))
        block.gpsimd(body("gp"))
        block.tensor(body("te"))
        block.vector(body("ve"))
        block.scalar(body("se"))
    return nc


_PROG_CACHE = {}


def kernel(L0_attn_x, L0_attn_w_qkv, L0_attn_w_out, L0_attn_b_out,
           L0_ffn_w_gate, L0_ffn_w_up, L0_ffn_w_down):
    x = np.ascontiguousarray(np.asarray(L0_attn_x, dtype=np.float32))
    wqkvT = np.ascontiguousarray(np.asarray(L0_attn_w_qkv, dtype=np.float32).T)
    woutT = np.ascontiguousarray(np.asarray(L0_attn_w_out, dtype=np.float32).T)
    brow = np.ascontiguousarray(np.asarray(L0_attn_b_out, dtype=np.float32).reshape(1, D))
    wgT = np.ascontiguousarray(np.asarray(L0_ffn_w_gate, dtype=np.int32).T)
    wuT = np.ascontiguousarray(np.asarray(L0_ffn_w_up, dtype=np.int32).T)
    wdT = np.ascontiguousarray(np.asarray(L0_ffn_w_down, dtype=np.int32).T)
    ident = np.eye(128, dtype=np.float32)
    mask = np.where(np.arange(128)[None, :] > np.arange(128)[:, None],
                    np.float32(NEG), np.float32(0.0)).astype(np.float32)

    if MODE not in _PROG_CACHE:
        _PROG_CACHE[MODE] = build_program(MODE)
    nc = _PROG_CACHE[MODE]

    in_maps = []
    for c in range(8):
        in_maps.append({
            "x": np.ascontiguousarray(x[c]),
            "wqkvT": wqkvT, "woutT": woutT, "brow": brow,
            "wgT": wgT, "wuT": wuT, "wdT": wdT,
            "ident": ident, "mask": mask,
        })
    res = run_bass_kernel_spmd(nc, in_maps, core_ids=list(range(8)))
    return np.stack([res.results[c]["out"] for c in range(8)], axis=0)
